# version 08
# speedup vs baseline: 1.0263x; 1.0147x over previous
import numpy as np
import ml_dtypes
import concourse.bass as bass
import concourse.mybir as mybir
from concourse.bass_utils import run_bass_kernel_spmd

F32 = mybir.dt.float32
BF16 = mybir.dt.bfloat16
U8 = mybir.dt.uint8
AF = mybir.ActivationFunctionType
ALU = mybir.AluOpType
AX = mybir.AxisListType
BF = ml_dtypes.bfloat16

B, T, C = 8, 4096, 1024
NH, HD, K, F4 = 16, 64, 16, 256
NC = T // K
NCORE = 8
NT = T // 128
EPS_LN = 1e-5
EPS_TTT = 1e-6
EPS_RMS = 1.1920929e-07
THETA = 10000.0

_CONST_CACHE = {}


def host_constants():
    if _CONST_CACHE:
        return _CONST_CACHE
    c = {}
    D = C
    jk = np.outer(np.arange(D, dtype=np.int64), np.arange(D, dtype=np.int64)) % D
    ang = 2.0 * np.pi * jk / D
    c["dft_cos"] = np.cos(ang).astype(BF)
    c["dft_sin"] = np.sin(ang).astype(BF)
    inv = 1.0 / THETA ** (np.arange(0, HD, 2, dtype=np.float64) / HD)
    pos = np.arange(K, dtype=np.float64)
    th = pos[None, :] * inv[:, None]
    cosP = np.zeros((128, K), np.float64)
    sinP = np.zeros((128, K), np.float64)
    for p in range(128):
        j = (p % HD) // 2
        cosP[p] = np.cos(th[j])
        sinP[p] = np.sin(th[j]) * (-1.0 if p % 2 == 0 else 1.0)
    c["rope_cos"] = cosP.astype(np.float32)
    c["rope_sin"] = sinP.astype(np.float32)
    c["ident"] = np.eye(128).astype(BF)
    _CONST_CACHE.update(c)
    return c



def build_l1():
    nc = bass.Bass("TRN2", target_bir_lowering=False)
    x = nc.dram_tensor("x", [T, C], F32, kind="ExternalInput").ap()
    wre = nc.dram_tensor("w_re", [C, C], F32, kind="ExternalInput").ap()
    wim = nc.dram_tensor("w_im", [C, C], F32, kind="ExternalInput").ap()
    dcos = nc.dram_tensor("dft_cos", [C, C], BF16, kind="ExternalInput").ap()
    dsin = nc.dram_tensor("dft_sin", [C, C], BF16, kind="ExternalInput").ap()
    xn_bf = nc.dram_tensor("xn_bf", [T, C], BF16, kind="ExternalOutput").ap()
    a_bf = nc.dram_tensor("a_bf", [C, C], BF16, kind="ExternalOutput").ap()

    xb = nc.alloc_sbuf_tensor("xb", [128, 2 * C], F32)
    xnb = nc.alloc_sbuf_tensor("xnb", [128, 2 * C], BF16)
    sqscr = nc.alloc_sbuf_tensor("sq", [128, C], F32)
    stx = nc.alloc_sbuf_tensor("stx", [128, 16], F32)
    wre_s = nc.alloc_sbuf_tensor("wre_s", [128, 8 * C], BF16)
    wim_s = nc.alloc_sbuf_tensor("wim_s", [128, 8 * C], BF16)
    wren_s = nc.alloc_sbuf_tensor("wren_s", [128, 8 * C], BF16)
    cos_s = nc.alloc_sbuf_tensor("cos_s", [128, 8 * C], BF16)
    sin_s = nc.alloc_sbuf_tensor("sin_s", [128, 8 * C], BF16)
    pt_s = nc.alloc_sbuf_tensor("pt_s", [128, 8 * C], BF16)
    qt_s = nc.alloc_sbuf_tensor("qt_s", [128, 8 * C], BF16)
    a_s = nc.alloc_sbuf_tensor("a_s", [128, 1024], BF16)
    ps0 = nc.alloc_psum_tensor("ps0", [128, 512], F32)
    ps1 = nc.alloc_psum_tensor("ps1", [128, 512], F32)
    ld = nc.alloc_semaphore("ld")
    ldg = nc.alloc_semaphore("ldg")
    dv = nc.alloc_semaphore("dv")
    ac = nc.alloc_semaphore("ac")
    pe = nc.alloc_semaphore("pe")
    st = nc.alloc_semaphore("st")
    with nc.Block() as block:
        ps = [ps0, ps1]
        LNA = 2 * NT
        LND = 6 * NT

        @block.sync
        def _(sync):
            sync.dma_start(out=cos_s[:, :].rearrange("p (a o) -> p a o", o=C),
                           in_=dcos.rearrange("(a p) o -> p a o", p=128)).then_inc(ld, 16)
            sync.dma_start(out=sin_s[:, :].rearrange("p (a o) -> p a o", o=C),
                           in_=dsin.rearrange("(a p) o -> p a o", p=128)).then_inc(ld, 16)
            for i in range(NT):
                if i >= 2:
                    sync.wait_ge(dv, 6 * (i - 2) + 6)
                sync.wait_ge(ld, 32 + 16 * i)
                sync.dma_start(out=xb[:, (i % 2) * C:(i % 2 + 1) * C],
                               in_=x[i * 128:(i + 1) * 128, :]).then_inc(ld, 16)

        @block.gpsimd
        def _(g):
            g.dma_start(out=wre_s[:, :].rearrange("p (a o) -> p a o", o=C),
                        in_=wre.rearrange("(a p) o -> p a o", p=128)).then_inc(ldg, 16)
            g.wait_ge(ldg, 16)
            g.dma_start(out=wim_s[:, :].rearrange("p (a o) -> p a o", o=C),
                        in_=wim.rearrange("(a p) o -> p a o", p=128)).then_inc(ldg, 16)
            for i in range(NT):
                g.wait_ge(dv, 6 * i + 6)
                g.wait_ge(st, 16 * i)
                g.dma_start(out=xn_bf[i * 128:(i + 1) * 128, :],
                            in_=xnb[:, (i % 2) * C:(i % 2 + 1) * C]).then_inc(st, 16)
            for j in range(16):
                g.wait_ge(ac, LNA + 32 + j + 1)
                g.wait_ge(st, 16 * (NT + j))
                g.dma_start(out=a_bf.rearrange("(dc p) o -> dc p o", p=128)[j // 2, :, (j % 2) * 512:(j % 2 + 1) * 512],
                            in_=a_s[:, (j % 2) * 512:(j % 2 + 1) * 512]).then_inc(st, 16)
            g.wait_ge(st, 16 * (NT + 16))

        @block.vector
        def _(v):
            for i in range(NT):
                b0 = (i % 2) * C
                d0 = 6 * i
                v.wait_ge(ld, 32 + 16 * (i + 1))
                if i >= 2:
                    v.wait_ge(st, 16 * (i - 1))
                v.reduce_sum(stx[:, 0:1], xb[:, b0:b0 + C], AX.X).then_inc(dv, 1)
                v.wait_ge(dv, d0 + 1)
                v.tensor_scalar_mul(stx[:, 1:2], stx[:, 0:1], 1.0 / C).then_inc(dv, 1)
                v.wait_ge(dv, d0 + 2)
                v.tensor_scalar(stx[:, 2:3], stx[:, 1:2], stx[:, 1:2], -EPS_LN,
                                ALU.mult, ALU.add).then_inc(dv, 1)
                v.wait_ge(dv, d0 + 3)
                v.wait_ge(ac, 2 * i + 1)
                v.scalar_tensor_tensor(stx[:, 3:4], stx[:, 4:5], 1.0 / C,
                                       stx[:, 2:3], ALU.mult, ALU.subtract).then_inc(dv, 1)
                v.wait_ge(ac, 2 * i + 2)
                v.reciprocal(stx[:, 6:7], stx[:, 5:6]).then_inc(dv, 1)
                v.wait_ge(dv, d0 + 5)
                v.tensor_scalar(xnb[:, b0:b0 + C], xb[:, b0:b0 + C],
                                stx[:, 1:2], stx[:, 6:7],
                                ALU.subtract, ALU.mult).then_inc(dv, 1)
            v.wait_ge(ldg, 32)
            v.tensor_scalar_mul(wren_s[:, :], wre_s[:, :], -1.0).then_inc(dv, 1)

        @block.scalar
        def _(s):
            for i in range(NT):
                b0 = (i % 2) * C
                s.wait_ge(ld, 32 + 16 * (i + 1))
                if i >= 1:
                    s.wait_ge(dv, 6 * (i - 1) + 4)
                s.activation(sqscr[:, :], xb[:, b0:b0 + C], AF.Square,
                             accum_out=stx[:, 4:5]).then_inc(ac, 1)
                s.wait_ge(dv, 6 * i + 4)
                s.activation(stx[:, 5:6], stx[:, 3:4], AF.Sqrt).then_inc(ac, 1)
            for n in range(48):
                s.wait_ge(pe, n + 1)
                src = ps[n % 2]
                if n < 16:
                    j = n
                    dst = pt_s[:, (j // 2) * C + (j % 2) * 512:(j // 2) * C + (j % 2 + 1) * 512]
                    s.activation(dst, src[:, :], AF.Copy).then_inc(ac, 1)
                elif n < 32:
                    j = n - 16
                    dst = qt_s[:, (j // 2) * C + (j % 2) * 512:(j // 2) * C + (j % 2 + 1) * 512]
                    s.activation(dst, src[:, :], AF.Copy, scale=-1.0).then_inc(ac, 1)
                else:
                    j = n - 32
                    if j >= 2:
                        s.wait_ge(st, 16 * (NT + j - 1))
                    s.activation(a_s[:, (j % 2) * 512:(j % 2 + 1) * 512],
                                 src[:, :], AF.Copy, scale=1.0 / C).then_inc(ac, 1)

        @block.tensor
        def _(t):
            t.wait_ge(ld, 32)
            t.wait_ge(ldg, 32)
            for n in range(48):
                ph, j = n // 16, n % 16
                oc, half = j // 2, j % 2
                if ph == 1 and j == 0:
                    t.wait_ge(dv, LND + 1)
                if ph == 2 and j == 0:
                    t.wait_ge(ac, LNA + 32)
                if n >= 2:
                    t.wait_ge(ac, LNA + n - 1)
                dst = ps[n % 2]
                if ph == 0:
                    lhsA, lhsB, rhsA, rhsB = wre_s, wim_s, cos_s, sin_s
                elif ph == 1:
                    lhsA, lhsB, rhsA, rhsB = wim_s, wren_s, cos_s, sin_s
                else:
                    lhsA, lhsB, rhsA, rhsB = pt_s, qt_s, cos_s, sin_s
                for kc in range(8):
                    t.matmul(dst[:, 0:512], lhsA[:, kc * C + oc * 128:kc * C + oc * 128 + 128],
                             rhsA[:, kc * C + half * 512:kc * C + (half + 1) * 512],
                             start=(kc == 0), stop=False)
                for kc in range(8):
                    last = kc == 7
                    mm = t.matmul(dst[:, 0:512], lhsB[:, kc * C + oc * 128:kc * C + oc * 128 + 128],
                                  rhsB[:, kc * C + half * 512:kc * C + (half + 1) * 512],
                                  start=False, stop=last)
                    if last:
                        mm.then_inc(pe, 1)
    return nc


def _prep_l1_inputs(inputs, core):
    cst = host_constants()
    return {
        "x": np.ascontiguousarray(inputs["L0_attn_hidden_states"][core]),
        "w_re": np.ascontiguousarray(inputs["L0_ffn_w_freq_real"]),
        "w_im": np.ascontiguousarray(inputs["L0_ffn_w_freq_imag"]),
        "dft_cos": cst["dft_cos"],
        "dft_sin": cst["dft_sin"],
    }



def build_l2():
    cst = host_constants()
    nc = bass.Bass("TRN2", target_bir_lowering=False)
    xn = nc.dram_tensor("xn_bf", [T, C], BF16, kind="ExternalInput").ap()
    wq = nc.dram_tensor("wq_t", [C, C], F32, kind="ExternalInput").ap()
    wk = nc.dram_tensor("wk_t", [C, C], F32, kind="ExternalInput").ap()
    wv = nc.dram_tensor("wv_t", [C, C], F32, kind="ExternalInput").ap()
    lrw = nc.dram_tensor("lrw_t", [C, 16], F32, kind="ExternalInput").ap()
    lrb = nc.dram_tensor("lrb", [16, 1], F32, kind="ExternalInput").ap()
    rc = nc.dram_tensor("rope_cos", [128, 16], F32, kind="ExternalInput").ap()
    rs = nc.dram_tensor("rope_sin", [128, 16], F32, kind="ExternalInput").ap()
    xq_fm = nc.dram_tensor("xq_fm", [C, T], BF16, kind="ExternalOutput").ap()
    xk_fm = nc.dram_tensor("xk_fm", [C, T], BF16, kind="ExternalOutput").ap()
    xv_fm = nc.dram_tensor("xv_fm", [C, T], BF16, kind="ExternalOutput").ap()
    lr_fm = nc.dram_tensor("lr_fm", [16, T], BF16, kind="ExternalOutput").ap()

    NTG = T // 512
    xnT = nc.alloc_sbuf_tensor("xnT", [128, 8 * T], BF16)
    wq_s = nc.alloc_sbuf_tensor("wq_s", [128, 8 * C], BF16)
    wk_s = nc.alloc_sbuf_tensor("wk_s", [128, 8 * C], BF16)
    wv_s = nc.alloc_sbuf_tensor("wv_s", [128, 8 * C], BF16)
    lrw_s = nc.alloc_sbuf_tensor("lrw_s", [128, 8 * 16], BF16)
    lrb_s = nc.alloc_sbuf_tensor("lrb_s", [16, 1], F32)
    rc_s = nc.alloc_sbuf_tensor("rc_s", [128, 16], F32)
    rs_s = nc.alloc_sbuf_tensor("rs_s", [128, 16], F32)
    shb = nc.alloc_sbuf_tensor("sh", [128, 2 * 512], F32)
    t1b = nc.alloc_sbuf_tensor("t1", [128, 2 * 512], F32)
    t2b = nc.alloc_sbuf_tensor("t2", [128, 512], F32)
    ob = nc.alloc_sbuf_tensor("ob", [128, 2 * 512], BF16)
    lrob = nc.alloc_sbuf_tensor("lrob", [16, 2 * 512], BF16)
    p0 = nc.alloc_psum_tensor("p0", [128, 512], F32)
    p1 = nc.alloc_psum_tensor("p1", [128, 512], F32)
    p2 = nc.alloc_psum_tensor("p2", [128, 512], F32)
    p3 = nc.alloc_psum_tensor("p3", [128, 512], F32)
    ld = nc.alloc_semaphore("ld")
    ldg = nc.alloc_semaphore("ldg")
    pe = nc.alloc_semaphore("pe")
    dv = nc.alloc_semaphore("dv")
    ac = nc.alloc_semaphore("ac")
    gp = nc.alloc_semaphore("gp")
    st = nc.alloc_semaphore("st")
    with nc.Block() as block:
        ps = [p0, p1, p2, p3]
        jobs = []
        for oc in range(8):
            for tg in range(NTG):
                jobs.append(("v", oc, tg))
        for w in ("q", "k"):
            for oc in range(8):
                for tg in range(NTG):
                    jobs.append((w, oc, tg))
        for tg in range(NTG):
            jobs.append(("lr", 0, tg))
        NJ = len(jobs)
        def plan():
            acn = dvn = gpn = stn = 0
            info = []
            for n, (kind, oc, tg) in enumerate(jobs):
                e = {"n": n, "kind": kind, "oc": oc, "tg": tg}
                if kind == "v":
                    e["ac_evac"] = acn = acn + 1
                    e["st"] = stn = stn + 1
                elif kind in ("q", "k"):
                    e["dv_sh"] = dvn + 1
                    e["dv_t1"] = dvn + 2
                    dvn += 2
                    e["gp_t2"] = gpn + 1
                    e["gp_out"] = gpn + 2
                    gpn += 2
                    e["st"] = stn = stn + 1
                else:
                    e["ac_evac"] = acn = acn + 1
                    e["st"] = stn = stn + 1
                info.append(e)
            return info
        info = plan()
        def psum_free_wait(eng, n):
            if n < 4:
                return
            e = info[n - 4]
            if "ac_evac" in e:
                eng.wait_ge(ac, e["ac_evac"])
            else:
                eng.wait_ge(dv, e["dv_t1"])

        @block.sync
        def _(sync):
            for oc in range(8):
                sync.wait_ge(ld, 16 * oc)
                sync.dma_start_transpose(xnT[:, oc * T:(oc + 1) * T],
                                         xn[:, oc * 128:(oc + 1) * 128]).then_inc(ld, 16)
            sync.wait_ge(ld, 16 * 8)
            sync.dma_start(out=rc_s[:, :], in_=rc[:, :]).then_inc(ld, 16)
            sync.wait_ge(ld, 16 * 9)
            sync.dma_start(out=rs_s[:, :], in_=rs[:, :]).then_inc(ld, 16)
            sync.wait_ge(ld, 16 * 10)
            sync.dma_start(out=lrb_s[:, :], in_=lrb[:, :]).then_inc(ld, 16)

        @block.gpsimd
        def _(g):
            for wi, (w_s, w) in enumerate(((wv_s, wv), (wq_s, wq), (wk_s, wk))):
                g.wait_ge(ldg, 16 * wi)
                g.dma_start(out=w_s[:, :].rearrange("p (a o) -> p a o", o=C),
                            in_=w.rearrange("(a p) o -> p a o", p=128)).then_inc(ldg, 16)
            g.wait_ge(ldg, 16 * 3)
            g.dma_start(out=lrw_s[:, :].rearrange("p (a o) -> p a o", o=16),
                        in_=lrw.rearrange("(a p) o -> p a o", p=128)).then_inc(ldg, 16)
            for e in info:
                n, kind, oc, tg = e["n"], e["kind"], e["oc"], e["tg"]
                if kind in ("q", "k"):
                    bsh = (n % 2) * 512
                    g.wait_ge(dv, e["dv_t1"])
                    g.tensor_mul(t2b[:, 0:512].rearrange("p (o t) -> p o t", t=16),
                                 shb[:, bsh:bsh + 512].rearrange("p (o t) -> p o t", t=16),
                                 rs_s[:, 0:16].rearrange("p (o t) -> p o t", o=1)
                                 .broadcast_to([128, 32, 16]))
                    if n >= 2:
                        g.wait_ge(st, 16 * (e["st"] - 2))
                    g.tensor_add(ob[:, bsh:bsh + 512], t1b[:, bsh:bsh + 512],
                                 t2b[:, 0:512]).then_inc(gp, 2)
                    g.wait_ge(gp, e["gp_out"])
                    g.wait_ge(st, 16 * (e["st"] - 1))
                    dst = xq_fm if kind == "q" else xk_fm
                    g.dma_start(out=dst[oc * 128:(oc + 1) * 128, tg * 512:(tg + 1) * 512],
                                in_=ob[:, bsh:bsh + 512]).then_inc(st, 16)
                elif kind == "v":
                    g.wait_ge(ac, e["ac_evac"])
                    g.wait_ge(st, 16 * (e["st"] - 1))
                    g.dma_start(out=xv_fm[oc * 128:(oc + 1) * 128, tg * 512:(tg + 1) * 512],
                                in_=ob[:, (n % 2) * 512:(n % 2) * 512 + 512]).then_inc(st, 16)
                else:
                    g.wait_ge(ac, e["ac_evac"])
                    g.wait_ge(st, 16 * (e["st"] - 1))
                    g.dma_start(out=lr_fm[:, tg * 512:(tg + 1) * 512],
                                in_=lrob[:, (n % 2) * 512:(n % 2) * 512 + 512]).then_inc(st, 16)
            g.wait_ge(st, 16 * NJ)

        @block.tensor
        def _(t):
            t.wait_ge(ld, 16 * 11)
            t.wait_ge(ldg, 16 * 4)
            for e in info:
                n, kind, oc, tg = e["n"], e["kind"], e["oc"], e["tg"]
                psum_free_wait(t, n)
                dst = ps[n % 4]
                if kind == "lr":
                    for kc in range(8):
                        mm = t.matmul(dst[0:16, 0:512], lrw_s[:, kc * 16:(kc + 1) * 16],
                                      xnT[:, kc * T + tg * 512:kc * T + (tg + 1) * 512],
                                      start=(kc == 0), stop=(kc == 7))
                        if kc == 7:
                            mm.then_inc(pe, 1)
                else:
                    w_s = {"q": wq_s, "k": wk_s, "v": wv_s}[kind]
                    for kc in range(8):
                        mm = t.matmul(dst[:, 0:512],
                                      w_s[:, kc * C + oc * 128:kc * C + oc * 128 + 128],
                                      xnT[:, kc * T + tg * 512:kc * T + (tg + 1) * 512],
                                      start=(kc == 0), stop=(kc == 7))
                        if kc == 7:
                            mm.then_inc(pe, 1)

        @block.vector
        def _(v):
            for e in info:
                n, kind = e["n"], e["kind"]
                if kind not in ("q", "k"):
                    continue
                bsh = (n % 2) * 512
                v.wait_ge(pe, n + 1)
                if n >= 2:
                    prev = info[n - 2]
                    if prev["kind"] in ("q", "k"):
                        v.wait_ge(gp, prev["gp_out"])
                src = ps[n % 4]
                swap = [i ^ 1 for i in range(32)]
                v.stream_shuffle(shb[:, bsh:bsh + 512], src[:, :], swap).then_inc(dv, 1)
                v.tensor_mul(t1b[:, bsh:bsh + 512].rearrange("p (o t) -> p o t", t=16),
                             src[:, :].rearrange("p (o t) -> p o t", t=16),
                             rc_s[:, 0:16].rearrange("p (o t) -> p o t", o=1)
                             .broadcast_to([128, 32, 16])).then_inc(dv, 1)

        @block.scalar
        def _(s):
            s.wait_ge(ld, 16 * 11)
            for e in info:
                n, kind = e["n"], e["kind"]
                if kind == "v":
                    s.wait_ge(pe, n + 1)
                    if n >= 2:
                        s.wait_ge(st, 16 * (e["st"] - 2))
                    s.activation(ob[:, (n % 2) * 512:(n % 2) * 512 + 512],
                                 ps[n % 4][:, :], AF.Copy).then_inc(ac, 1)
                elif kind == "lr":
                    s.wait_ge(pe, n + 1)
                    if n >= 2:
                        prev = info[n - 2]
                        if prev["kind"] == "lr":
                            s.wait_ge(st, 16 * (prev["st"]))
                    s.activation(lrob[:, (n % 2) * 512:(n % 2) * 512 + 512],
                                 ps[n % 4][0:16, :], AF.Sigmoid,
                                 bias=lrb_s[:, 0:1]).then_inc(ac, 1)
    return nc


def _prep_l2_inputs(inputs, core, xn_bf):
    cst = host_constants()
    return {
        "xn_bf": xn_bf,
        "wq_t": np.ascontiguousarray(inputs["L0_attn_wq"].T),
        "wk_t": np.ascontiguousarray(inputs["L0_attn_wk"].T),
        "wv_t": np.ascontiguousarray(inputs["L0_attn_wv"].T),
        "lrw_t": np.ascontiguousarray(inputs["L0_attn_ttt_lr_weight"].reshape(16, C).T),
        "lrb": np.ascontiguousarray(inputs["L0_attn_ttt_lr_bias"].reshape(16, 1)),
        "rope_cos": cst["rope_cos"],
        "rope_sin": cst["rope_sin"],
    }



_ENGS = ("pe", "dv", "ac", "gp", "sd", "gd")
_DMA_ENGS = ("sd", "gd")


class _Op:
    __slots__ = ("eng", "emit", "deps", "inc_no", "need_inc", "small", "tag", "sem_ep")


class Buf:
    __slots__ = ("name", "last_w", "readers")

    def __init__(self, name):
        self.name = name
        self.last_w = None
        self.readers = []


class Sched:
    def __init__(self):
        self.ops = []
        self.limit = None
        self.current_barrier = []

    def op(self, eng, emit, reads=(), writes=(), extra=(), small=False, tag=""):
        if self.limit is not None and len(self.ops) >= self.limit:
            return None
        if eng in ("dv", "ac", "gp"):
            small = True
        o = _Op()
        o.eng, o.emit, o.small, o.tag = eng, emit, small, tag
        deps = []
        for b in reads:
            if b.last_w is not None:
                deps.append(b.last_w)
        for b in writes:
            deps.extend(b.readers)
            if b.last_w is not None:
                deps.append(b.last_w)
        deps.extend(extra)
        if self.current_barrier and not tag.startswith("bar_"):
            deps.extend(self.current_barrier)
        seen = set()
        o.deps = []
        for d in deps:
            if d is not None and id(d) not in seen:
                seen.add(id(d))
                o.deps.append(d)
        self.ops.append(o)
        for b in reads:
            b.readers.append(o)
        for b in writes:
            b.readers = [o]
            b.last_w = o
        return o

    SEM_LIM = 30000

    def finalize(self):
        for o in self.ops:
            o.need_inc = False
            o.inc_no = 0
            o.sem_ep = 0
        for o in self.ops:
            for d in o.deps:
                d.need_inc = True
        cnt = {e: 0 for e in _ENGS}
        ep = {e: 0 for e in _ENGS}
        for o in self.ops:
            if o.eng in _DMA_ENGS:
                o.need_inc = True
            if o.need_inc:
                step = 16 if o.eng in _DMA_ENGS else 1
                if cnt[o.eng] + step > self.SEM_LIM:
                    ep[o.eng] += 1
                    cnt[o.eng] = 0
                cnt[o.eng] += step
                o.inc_no = cnt[o.eng]
                o.sem_ep = ep[o.eng]
        self.final_counts = cnt
        self.n_epochs = {e: ep[e] + 1 for e in _ENGS}

    def emit_engine(self, engines, sems, queue):
        eng = engines[queue]
        my = {"tensor": ("pe",), "vector": ("dv",), "scalar": ("ac",),
              "sync": ("sd",), "gpsimd": ("gp", "gd")}[queue]
        waited = {e: (-1, 0) for e in _ENGS}
        last_dma = {e: (0, 0) for e in _DMA_ENGS}

        def need_wait(e, ep_, v):
            wep, wv = waited[e]
            return ep_ > wep or (ep_ == wep and v > wv)

        for o in self.ops:
            if o.eng not in my:
                continue
            if o.eng in _DMA_ENGS:
                dep_, dv_ = last_dma[o.eng]
                if dv_ > 0 and need_wait(o.eng, dep_, dv_):
                    eng.wait_ge(sems[(o.eng, dep_)], dv_)
                    waited[o.eng] = (dep_, dv_)
            for d in o.deps:
                if d.eng == o.eng and not (d.small or d.eng in _DMA_ENGS):
                    continue
                if d.eng in my and d.eng not in _DMA_ENGS and not d.small:
                    continue
                if d.inc_no > 0 and need_wait(d.eng, d.sem_ep, d.inc_no):
                    eng.wait_ge(sems[(d.eng, d.sem_ep)], d.inc_no)
                    waited[d.eng] = (d.sem_ep, d.inc_no)
            instr = o.emit(eng)
            if o.need_inc:
                assert instr is not None, f"op {o.tag} needs inc but emitted None"
                instr.then_inc(sems[(o.eng, o.sem_ep)], 16 if o.eng in _DMA_ENGS else 1)
            if o.eng in _DMA_ENGS:
                last_dma[o.eng] = (o.sem_ep, o.inc_no)

    def final_wait(self, eng, sems):
        for e in _ENGS:
            if self.final_counts[e]:
                eng.wait_ge(sems[(e, self.n_epochs[e] - 1)], self.final_counts[e])


def run_sched(nc, sched):
    sched.finalize()
    sems = {(e, k): nc.alloc_semaphore(f"sem_{e}{k}")
            for e in _ENGS for k in range(sched.n_epochs[e])}
    with nc.Block() as block:
        @block.sync
        def _(sync):
            sched.emit_engine({"sync": sync}, sems, "sync")

        @block.tensor
        def _(t):
            sched.emit_engine({"tensor": t}, sems, "tensor")

        @block.vector
        def _(v):
            sched.emit_engine({"vector": v}, sems, "vector")

        @block.scalar
        def _(s):
            sched.emit_engine({"scalar": s}, sems, "scalar")

        @block.gpsimd
        def _(g):
            sched.emit_engine({"gpsimd": g}, sems, "gpsimd")
            sched.final_wait(g, sems)
    return nc



CPB = 32
TBLK = 512
NBLK = T // TBLK

GC1 = 2 * 0.79788456 * 0.044715
GC2 = 2 * 0.79788456
GB1 = -2 * 0.1070322243
GB2 = -2 * 0.79788456


def _l3_consts():
    c = {}
    c["id32"] = np.vstack([np.eye(32)] * 4).astype(BF)
    selh = np.zeros((16, 256), np.float32)
    for hg in range(2):
        for h8 in range(8):
            selh[hg * 8 + h8, hg * 128 + h8 * 16:hg * 128 + h8 * 16 + 16] = 1.0
    c["selh"] = selh.astype(BF)
    dp = np.zeros((128, 512), np.float32)
    for p in range(128):
        dp[p, p % 16::16] = 1.0 / HD
    c["dpat"] = dp.astype(BF)
    po = np.zeros((128, 16), np.float32)
    for p in range(128):
        po[p, p // 16] = 1.0
    c["pones"] = po.astype(BF)
    rp = np.zeros((16, 128), np.float32)
    for r in range(16):
        rp[r, r * 8:(r + 1) * 8] = 0.0
    rp = np.zeros((16, 128), np.float32)
    for p in range(128):
        rp[p // 16, p] = 1.0
    c["rep16"] = rp.astype(BF)
    mE = np.zeros((128, 64), np.float32)
    mO = np.zeros((128, 64), np.float32)
    for p in range(128):
        if (p // 16) % 2 == 0:
            mE[p] = 1.0
        else:
            mO[p] = 1.0
    c["mE"], c["mO"] = mE, mO
    c["selu8"] = (mE[:, :64] > 0).astype(np.uint8)
    maskT = (np.arange(K)[:, None] <= np.arange(K)[None, :]).astype(np.float32)
    negmb2 = np.zeros((32, 64), np.float32)
    for hh in range(2):
        negmb2[hh * 16:(hh + 1) * 16, hh * 16:(hh + 1) * 16] = -maskT
        negmb2[hh * 16:(hh + 1) * 16, 32 + hh * 16:32 + hh * 16 + 16] = -maskT
    c["negmb2"] = np.tile(negmb2, (4, 1))
    c["invk"] = (1.0 / np.arange(1, K + 1, dtype=np.float32)).reshape(1, K)
    c["onesb"] = np.ones((1, 128), np.float32).astype(BF)
    return c


_L3C = None


def l3_consts():
    global _L3C
    if _L3C is None:
        _L3C = _l3_consts()
    return _L3C


def build_l3(nblk=NBLK, cpb=CPB, oplim=None, debug=False, cx=None, blk_hook=None):
    standalone = cx is None
    if standalone:
        cx = Ctx(bass.Bass("TRN2", target_bir_lowering=False), False)
        cx.S = Sched()
        cx.S.limit = oplim
        cx.phase("l3")
    nc = cx.nc
    dq = cx.dram_inter("xq_fm", [C, T], BF16, "in")
    dk = cx.dram_inter("xk_fm", [C, T], BF16, "in")
    dv_ = cx.dram_inter("xv_fm", [C, T], BF16, "in")
    dlr = cx.dram_inter("lr_fm", [16, T], BF16, "in")
    dw1 = cx.dram_in("w1p", [8, 128, 256], F32)
    dw2t = cx.dram_in("w2tp", [8, 128, 256], F32)
    dw2c = cx.dram_in("w2cp", [8, 2, 128, 128], F32)
    db1 = cx.dram_in("b1b0", [2, 128, 256], F32)
    db2 = cx.dram_in("b2b0", [2, 128, 64], F32)
    dlnw = cx.dram_in("lnw8", [2, 128, 64], F32)
    dlnb = cx.dram_in("lnb8", [2, 128, 64], F32)
    dlti = cx.dram_in("lti", [1, 16], F32)
    dident = cx.dram_in("identb", [128, 128], BF16)
    did32 = cx.dram_in("id32", [128, 32], BF16)
    dselh = cx.dram_in("selh", [16, 256], BF16)
    ddpat = cx.dram_in("dpat", [128, 512], BF16)
    dpones = cx.dram_in("pones", [128, 16], BF16)
    drep16 = cx.dram_in("rep16", [16, 128], BF16)
    dmE = cx.dram_in("mE", [128, 64], F32)
    dmO = cx.dram_in("mO", [128, 64], F32)
    dselu8 = cx.dram_in("selu8", [128, 64], U8)
    dnegmb = cx.dram_in("negmb2", [128, 64], F32)
    dinvk = cx.dram_in("invk", [1, 16], F32)
    donesb = cx.dram_in("onesb", [1, 128], BF16)
    dy = cx.dram_inter("y_tm", [T, C], BF16, "out")
    ddbg = nc.dram_tensor("dbg", [128, 16384], F32, kind="ExternalOutput").ap() if debug else None

    A = cx.sb
    qfm = A("qfm", [128, 8 * TBLK], BF16)
    kfm = A("kfm", [128, 8 * TBLK], BF16)
    vfm = A("vfm", [128, 8 * TBLK], BF16)
    qht = A("qht", [128, 8 * TBLK], BF16)
    kht = A("kht", [128, 8 * TBLK], BF16)
    tght = A("tght", [128, 8 * TBLK], BF16)
    bdq = A("bdq", [128, 8192], BF16)
    bdk = A("bdk", [128, 8192], BF16)
    lrb = A("lrb", [16, TBLK], BF16)
    lrf = A("lrf", [128, 64], F32)
    sw1 = A("sw1", [128, 8 * 256], BF16)
    sw2t = A("sw2t", [128, 8 * 256], BF16)
    w2c = A("w2c", [128, 8 * 256], BF16)
    b1b = A("b1b", [128, 512], F32)
    b2b = A("b2b", [128, 128], F32)
    gB = A("gBt", [128, 128], F32)
    bB = A("bBt", [128, 128], F32)
    g2B = A("g2B", [128, 128], F32)
    gbB = A("gbB", [128, 128], F32)
    identb = A("identb_s", [128, 128], BF16)
    id32 = A("id32s", [128, 32], BF16)
    selh = A("selhs", [16, 256], BF16)
    dpat = A("dpats", [128, 512], BF16)
    pones = A("poness", [128, 16], BF16)
    rep16 = A("rep16s", [16, 128], BF16)
    mE = A("mEs", [128, 64], F32)
    mO = A("mOs", [128, 64], F32)
    selu8 = A("selu8s", [128, 64], U8)
    negmb = A("negmbs", [128, 64], F32)
    invk = A("invks", [1, 16], F32)
    onesb = A("onesbs", [1, 128], BF16)
    tokrow = A("tokrow", [1, 16], F32)
    tokbf = A("tokbf", [1, 16], BF16)
    tokrep = A("tokrep", [1, 64], BF16)
    nmsb = A("nmsb", [128, 64], BF16)
    nt15 = A("nt15", [128, 1], F32)
    xqT8 = A("xqT8", [128, 128], BF16)
    xkT8 = A("xkT8", [128, 128], BF16)
    tgT8 = A("tgT8", [128, 128], BF16)
    kst = A("kst", [128, 256], BF16)
    z1t = A("z1t", [128, 512], F32)
    sq1 = A("sq1", [128, 512], F32)
    sg1 = A("sg1", [128, 512], F32)
    scr = A("scr", [128, 512], F32)
    x2 = A("x2", [128, 512], BF16)
    x2f = A("x2f", [128, 512], BF16)
    z2s = A("z2s", [128, 256], F32)
    z2t = A("z2t", [128, 128], F32)
    lnst = A("lnst", [128, 16], F32)
    xh = A("xhb", [128, 128], F32)
    gop = A("gop", [128, 128], F32)
    t2g = A("t2g", [128, 128], F32)
    u1b = A("u1b", [128, 128], F32)
    ldx = A("ldx", [128, 128], F32)
    gst = A("gst", [128, 256], BF16)
    gstp = A("gstp", [128, 256], BF16)
    gbd = A("gbd", [128, 256], BF16)
    m5s = A("m5s", [128, 64], BF16)
    m7s = A("m7s", [128, 64], BF16)
    gz1p = A("gz1p", [128, 512], BF16)
    z1bt = A("z1bt", [128, 512], F32)
    x2b = A("x2b", [128, 512], BF16)
    x2bf = A("x2bf", [128, 512], BF16)
    z2bt = A("z2bt", [128, 128], F32)
    xh2 = A("xh2", [128, 128], F32)
    bsum = A("bsum", [16, 512], BF16)
    bde = A("bde", [128, 128], F32)
    yblk = A("yblk", [128, CPB * 128], BF16)

    P = cx.psum
    pz1 = P("pz1", [128, 512], F32)
    pz1b = P("pz1b", [128, 512], F32)
    pV = P("pV", [128, 512], F32)
    pmx = P("pmx", [128, 512], F32)
    pbf = P("pbf", [128, 1024], BF16)
    pd0 = P("pd0", [128, 512], F32)
    pd1 = P("pd1", [128, 512], F32)
    pd2 = P("pd2", [128, 512], F32)

    S = cx.S
    gb = cx.gb

    def dmaop(eng, dst, src, tag):
        return S.op(eng, lambda e, d=dst, s=src: e.dma_start(out=d, in_=s),
                    writes=(gb(tag),), tag=tag)

    def tpos(row, col):
        if row or col:
            return {"tile_position": (row, col)}
        return {}

    S.op("gd", lambda e: e.dma_start(out=sw1[:, :].rearrange("p (g f) -> p g f", f=256),
                                     in_=dw1.rearrange("g p f -> p g f")),
         writes=(gb("sw1"),), tag="ld_sw1")
    S.op("gd", lambda e: e.dma_start(out=sw2t[:, :].rearrange("p (g f) -> p g f", f=256),
                                     in_=dw2t.rearrange("g p f -> p g f")),
         writes=(gb("sw2t"),), tag="ld_sw2t")
    S.op("gd", lambda e: e.dma_start(out=w2c[:, :].rearrange("p (g c f) -> p g c f", c=2, f=128),
                                     in_=dw2c.rearrange("g c p f -> p g c f")),
         writes=(gb("w2c"),), tag="ld_w2c")
    S.op("sd", lambda e: e.dma_start(out=b1b[:, :].rearrange("p (hg f) -> p hg f", f=256),
                                     in_=db1.rearrange("hg p f -> p hg f")),
         writes=(gb("b1b"),), tag="ld_b1b")
    S.op("sd", lambda e: e.dma_start(out=b2b[:, :].rearrange("p (hg f) -> p hg f", f=64),
                                     in_=db2.rearrange("hg p f -> p hg f")),
         writes=(gb("b2b"),), tag="ld_b2b")
    S.op("sd", lambda e: e.dma_start(out=gB[:, :].rearrange("p (hg f) -> p hg f", f=64),
                                     in_=dlnw.rearrange("hg p f -> p hg f")),
         writes=(gb("gB"),), tag="ld_gB")
    S.op("sd", lambda e: e.dma_start(out=bB[:, :].rearrange("p (hg f) -> p hg f", f=64),
                                     in_=dlnb.rearrange("hg p f -> p hg f")),
         writes=(gb("bB"),), tag="ld_bB")
    for dst, src, tag in ((identb, dident, "identb"), (id32, did32, "id32"),
                          (selh, dselh, "selh"), (dpat, ddpat, "dpat"),
                          (pones, dpones, "pones"), (rep16, drep16, "rep16"),
                          (mE, dmE, "mE"), (mO, dmO, "mO"), (selu8, dselu8, "selu8"),
                          (negmb, dnegmb, "negmb"), (invk, dinvk, "invk"),
                          (onesb, donesb, "onesb")):
        dmaop("sd", dst[:, :], src[:, :], "c_" + tag)
    S.op("sd", lambda e: e.dma_start(out=tokrow[:, :], in_=dlti[:, :]),
         writes=(gb("tokrow"),), tag="ld_lti")

    S.op("dv", lambda e: e.tensor_add(tokrow[:, :], tokrow[:, :], invk[:, :]),
         reads=(gb("tokrow"), gb("c_invk")), writes=(gb("tokrow"),), small=True, tag="tok_add")
    S.op("dv", lambda e: e.tensor_scalar_max(tokrow[:, :], tokrow[:, :], 0.0),
         reads=(gb("tokrow"),), writes=(gb("tokrow"),), small=True, tag="tok_max")
    S.op("dv", lambda e: e.tensor_copy(tokbf[:, :], tokrow[:, :]),
         reads=(gb("tokrow"),), writes=(gb("tokbf"),), small=True, tag="tok_bf")
    S.op("dv", lambda e: e.tensor_copy(
            tokrep[:, :].rearrange("p (a t) -> p a t", t=16),
            tokbf[:, :].rearrange("p (a t) -> p a t", a=1).broadcast_to([1, 4, 16])),
         reads=(gb("tokbf"),), writes=(gb("tokrep"),), small=True, tag="tok_rep")
    S.op("pe", lambda e: e.matmul(pmx[0:128, 256:320], onesb[0:1, 0:128], tokrep[0:1, :],
                                  start=True, stop=True),
         reads=(gb("tokrep"), gb("c_onesb")), writes=(gb("pmx"),), tag="nm_mm")
    S.op("dv", lambda e: e.tensor_mul(nmsb[:, :], pmx[0:128, 256:320], negmb[:, :]),
         reads=(gb("pmx"), gb("c_negmb")), writes=(gb("nmsb"),), tag="nm_tt")
    S.op("pe", lambda e: e.matmul(pd0[0:128, 0:1], onesb[0:1, 0:128], tokbf[0:1, 15:16],
                                  start=True, stop=True),
         reads=(gb("tokbf"), gb("c_onesb")), writes=(gb("pd0"),), tag="nt15_mm")
    S.op("ac", lambda e: e.activation(nt15[:, :], pd0[0:128, 0:1], AF.Copy, scale=-1.0),
         reads=(gb("pd0"),), writes=(gb("nt15"),), small=True, tag="nt15_ev")
    S.op("gp", lambda e: e.tensor_mul(g2B[:, :], gB[:, :], gB[:, :]),
         reads=(gb("gB"),), writes=(gb("g2B"),), tag="g2B")
    S.op("gp", lambda e: e.tensor_mul(gbB[:, :], gB[:, :], bB[:, :]),
         reads=(gb("gB"), gb("bB")), writes=(gb("gbB"),), tag="gbB")
    S.op("dv", lambda e: e.memset(bdq[:, :], 0.0), writes=(gb("bdq"),), tag="z_bdq")
    S.op("dv", lambda e: e.memset(bdk[:, :], 0.0), writes=(gb("bdk"),), tag="z_bdk")

    ID = identb
    PDW = [gb("pd0"), gb("pd1"), gb("pd2")]
    PDWT = [pd0, pd1, pd2]

    for bi in range(nblk):
        t0 = bi * TBLK
        if blk_hook is not None and bi > 0:
            blk_hook(bi - 1)
        for dst, src, tag in ((qfm, dq, "qfm"), (kfm, dk, "kfm"), (vfm, dv_, "vfm")):
            S.op("sd", lambda e, d=dst, s=src, t0=t0: e.dma_start(
                    out=d[:, :].rearrange("p (g t) -> p g t", t=TBLK),
                    in_=s.rearrange("(g p) t -> p g t", p=128)[:, :, t0:t0 + TBLK]),
                 writes=(gb(tag),), tag=tag + str(bi))

        S.op("sd", lambda e, t0=t0: e.dma_start(out=lrb[:, :], in_=dlr[:, t0:t0 + TBLK]),
             writes=(gb("lrb"),), tag="lrb" + str(bi))
        for h in range(16):
            hg, h8 = h // 8, h % 8
            g, hh = h // 2, h % 2
            for src, dst, tg_, sb in ((qfm, qht, "qht", "qfm"), (kfm, kht, "kht", "kfm"),
                                      (vfm, tght, "tght", "vfm")):
                S.op("dv", lambda e, s=src, d=dst, hg=hg, h8=h8, g=g, hh=hh: e.tensor_copy(
                        d[hg * 64:(hg + 1) * 64, :]
                         .rearrange("p (n w) -> p n w", w=128)[:, :, h8 * 16:h8 * 16 + 16],
                        s[hh * 64:(hh + 1) * 64, g * TBLK:(g + 1) * TBLK]
                         .rearrange("p (n t) -> p n t", t=16)),
                     reads=(gb(sb),), writes=(gb(tg_),), tag=f"{tg_}{bi}_{h}")
        S.op("gp", lambda e: e.tensor_sub(tght[:, :], tght[:, :], kht[:, :]),
             reads=(gb("tght"), gb("kht")), writes=(gb("tght"),), tag=f"tgt{bi}")
        for g in range(8):
            for hh in range(2):
                for src, dst, tg_, sb in ((qfm, bdq, "bdq", "qfm"), (kfm, bdk, "bdk", "kfm")):
                    S.op("dv", lambda e, s=src, d=dst, g=g, hh=hh: e.tensor_copy(
                            d[hh * 64:(hh + 1) * 64, g * 1024:(g + 1) * 1024]
                             .rearrange("p (n w) -> p n w", w=32)[:, :, hh * 16:hh * 16 + 16],
                            s[hh * 64:(hh + 1) * 64, g * TBLK:(g + 1) * TBLK]
                             .rearrange("p (n t) -> p n t", t=16)),
                         reads=(gb(sb),), writes=(gb(tg_),), tag=f"{tg_}{bi}_{g}_{hh}")
        for hg in range(2):
            S.op("pe", lambda e, hg=hg: e.matmul(pz1[0:128, 0:512],
                                                 selh[:, hg * 128:(hg + 1) * 128],
                                                 lrb[0:16, :], start=True, stop=True),
                 reads=(gb("lrb"), gb("c_selh")), writes=(gb("pz1"),), tag=f"lrmm{bi}_{hg}")
            S.op("dv", lambda e: e.tensor_mul(scr[:, :], pz1[:, 0:512], dpat[:, :]),
                 reads=(gb("pz1"), gb("c_dpat")), writes=(gb("scr"),), tag=f"lrtt{bi}_{hg}")
            S.op("dv", lambda e, hg=hg: e.tensor_reduce(
                    lrf[:, hg * 32:(hg + 1) * 32],
                    scr[:, :].rearrange("p (n t) -> p n t", t=16), AX.X, ALU.add),
                 reads=(gb("scr"),), writes=(gb("lrf"),), tag=f"lrrd{bi}_{hg}")

        for n in range(cpb):
            tagn = f"b{bi}c{n}"
            for i, (src, dsts, srcb, dstb) in enumerate(
                    ((qht, xqT8, "qht", "xqT8"), (kht, xkT8, "kht", "xkT8"),
                     (tght, tgT8, "tght", "tgT8"))):
                S.op("pe", lambda e, s=src, i=i, n=n: e.transpose(
                        pbf[:, i * 128:(i + 1) * 128],
                        s[:, 128 * n:128 * n + 128], ID[:, :]),
                     reads=(gb(srcb), gb("c_identb")), writes=(gb("pbf"),), tag=f"T1_{i}_{tagn}")
                S.op("ac", lambda e, d=dsts, i=i: e.activation(
                        d[:, :], pbf[:, i * 128:(i + 1) * 128], AF.Copy),
                     reads=(gb("pbf"),), writes=(gb(dstb),), tag=f"E1_{i}_{tagn}")
            for hg in range(2):
                for half, m, mb in ((0, mE, "c_mE"), (1, mO, "c_mO")):
                    S.op("gp", lambda e, hg=hg, half=half, m=m: e.tensor_mul(
                            kst[:, hg * 128 + half * 64:hg * 128 + half * 64 + 64],
                            xkT8[:, hg * 64:(hg + 1) * 64], m[:, :]),
                         reads=(gb("xkT8"), gb(mb)), writes=(gb("kst"),),
                         tag=f"kst{hg}{half}_{tagn}")
            for g in range(8):
                hg, s_ = g // 4, g % 4
                S.op("pe", lambda e, g=g, hg=hg, s_=s_, n=n: e.matmul(
                        pz1[32 * s_:32 * s_ + 32, hg * 256:hg * 256 + 256],
                        bdk[:, g * 1024 + 32 * n:g * 1024 + 32 * n + 32],
                        sw1[:, g * 256:(g + 1) * 256], start=True, stop=True,
                        **tpos(0, 32 * s_)),
                     reads=(gb("bdk"), gb("sw1")), writes=(gb("pz1"),), tag=f"M1_{g}_{tagn}")
            for g in range(8):
                hg, s_ = g // 4, g % 4
                S.op("pe", lambda e, g=g, hg=hg, s_=s_, n=n: e.matmul(
                        pz1b[32 * s_:32 * s_ + 32, hg * 256:hg * 256 + 256],
                        bdq[:, g * 1024 + 32 * n:g * 1024 + 32 * n + 32],
                        sw1[:, g * 256:(g + 1) * 256], start=True, stop=True,
                        **tpos(0, 32 * s_)),
                     reads=(gb("bdq"), gb("sw1")), writes=(gb("pz1b"),), tag=f"M4_{g}_{tagn}")
            for g in range(8):
                hg, s_ = g // 4, g % 4
                S.op("pe", lambda e, g=g, hg=hg, s_=s_, n=n: e.matmul(
                        pmx[32 * s_:32 * s_ + 32, 256 + hg * 32:256 + hg * 32 + 32],
                        bdk[:, g * 1024 + 32 * n:g * 1024 + 32 * n + 32],
                        bdq[:, g * 1024 + 32 * n:g * 1024 + 32 * n + 32],
                        start=True, stop=True, **tpos(0, 32 * s_)),
                     reads=(gb("bdk"), gb("bdq")), writes=(gb("pmx"),), tag=f"MA1_{g}_{tagn}")
            S.op("dv", lambda e: e.tensor_add(z1t[:, :], pz1[:, :], b1b[:, :]),
                 reads=(gb("pz1"), gb("b1b")), writes=(gb("z1t"),), tag=f"V1_{tagn}")
            S.op("dv", lambda e: e.scalar_tensor_tensor(m5s[:, :], pmx[0:128, 256:320], 1.0,
                                                        nmsb[:, :], ALU.add, ALU.mult),
                 reads=(gb("pmx"), gb("nmsb")), writes=(gb("m5s"),), tag=f"M5b_{tagn}")
            S.op("ac", lambda e: e.activation(sq1[:, :], z1t[:, :], AF.Square),
                 reads=(gb("z1t"),), writes=(gb("sq1"),), tag=f"E2_{tagn}")
            S.op("gp", lambda e: e.tensor_scalar(scr[:, :], sq1[:, :], GC1, GC2,
                                                 ALU.mult, ALU.add),
                 reads=(gb("sq1"),), writes=(gb("scr"),), tag=f"S2_{tagn}")
            S.op("gp", lambda e: e.tensor_mul(scr[:, :], scr[:, :], z1t[:, :]),
                 reads=(gb("scr"), gb("z1t")), writes=(gb("scr"),), tag=f"S3_{tagn}")
            S.op("ac", lambda e: e.activation(sg1[:, :], scr[:, :], AF.Sigmoid),
                 reads=(gb("scr"),), writes=(gb("sg1"),), tag=f"E3_{tagn}")
            S.op("dv", lambda e: e.tensor_mul(x2[:, :], sg1[:, :], z1t[:, :]),
                 reads=(gb("sg1"), gb("z1t")), writes=(gb("x2"),), tag=f"V2_{tagn}")
            for i in range(4):
                hg, fc = i // 2, i % 2
                sl = 640 + (i % 2) * 128
                S.op("pe", lambda e, hg=hg, fc=fc, sl=sl: e.transpose(
                        pbf[:, sl:sl + 128],
                        x2[:, hg * 256 + fc * 128:hg * 256 + fc * 128 + 128], ID[:, :]),
                     reads=(gb("x2"), gb("c_identb")), writes=(gb("pbf"),), tag=f"T2_{i}_{tagn}")
                S.op("ac", lambda e, hg=hg, fc=fc, sl=sl: e.activation(
                        x2f[:, fc * 256 + hg * 128:fc * 256 + hg * 128 + 128],
                        pbf[:, sl:sl + 128], AF.Copy),
                     reads=(gb("pbf"),), writes=(gb("x2f"),), tag=f"E4_{i}_{tagn}")
            for g in range(8):
                hg, s_ = g // 4, g % 4
                for fc in range(2):
                    S.op("pe", lambda e, g=g, hg=hg, s_=s_, fc=fc: e.matmul(
                            pmx[32 * s_:32 * s_ + 32, hg * 128:hg * 128 + 128],
                            x2f[:, fc * 256 + hg * 128 + 32 * s_:fc * 256 + hg * 128 + 32 * s_ + 32],
                            w2c[:, g * 256 + fc * 128:g * 256 + fc * 128 + 128],
                            start=(fc == 0), stop=(fc == 1), **tpos(0, 32 * s_)),
                         reads=(gb("x2f"), gb("w2c")), writes=(gb("pmx"),),
                         tag=f"M2_{g}_{fc}_{tagn}")
            S.op("ac", lambda e: e.activation(z2s[:, :], pmx[:, 0:256], AF.Copy),
                 reads=(gb("pmx"),), writes=(gb("z2s"),), tag=f"E5_{tagn}")
            for hg in range(2):
                S.op("dv", lambda e, hg=hg: e.select(
                        z2t[:, hg * 64:(hg + 1) * 64], selu8[:, :],
                        z2s[:, hg * 128:hg * 128 + 64], z2s[:, hg * 128 + 64:hg * 128 + 128]),
                     reads=(gb("z2s"), gb("c_selu8")), writes=(gb("z2t"),),
                     tag=f"V3a{hg}_{tagn}")
            S.op("dv", lambda e: e.tensor_add(z2t[:, :], z2t[:, :], b2b[:, :]),
                 reads=(gb("z2t"), gb("b2b")), writes=(gb("z2t"),), tag=f"V3b_{tagn}")
            S.op("gp", lambda e: e.tensor_mul(scr[:, 0:128], z2t[:, :], z2t[:, :]),
                 reads=(gb("z2t"),), writes=(gb("scr"),), tag=f"S_sq_{tagn}")
            S.op("dv", lambda e: e.tensor_reduce(lnst[:, 0:2],
                    scr[:, 0:128].rearrange("p (hg d) -> p hg d", d=64), AX.X, ALU.add),
                 reads=(gb("scr"),), writes=(gb("lnst"),), small=True, tag=f"V_ssq_{tagn}")
            S.op("dv", lambda e: e.tensor_reduce(lnst[:, 2:4],
                    z2t[:, :].rearrange("p (hg d) -> p hg d", d=64), AX.X, ALU.add),
                 reads=(gb("z2t"),), writes=(gb("lnst"),), small=True, tag=f"V_s1_{tagn}")
            S.op("dv", lambda e: e.tensor_scalar_mul(lnst[:, 4:6], lnst[:, 2:4], 1.0 / 64),
                 reads=(gb("lnst"),), writes=(gb("lnst"),), small=True, tag=f"V_mu_{tagn}")
            S.op("dv", lambda e: e.tensor_tensor(lnst[:, 6:8], lnst[:, 4:6], lnst[:, 4:6],
                                                 ALU.mult),
                 reads=(gb("lnst"),), writes=(gb("lnst"),), small=True, tag=f"V_m2_{tagn}")
            S.op("dv", lambda e: e.tensor_scalar(lnst[:, 6:8], lnst[:, 6:8], 1.0, -EPS_TTT,
                                                 ALU.mult, ALU.add),
                 reads=(gb("lnst"),), writes=(gb("lnst"),), small=True, tag=f"V_m2e_{tagn}")
            S.op("dv", lambda e: e.scalar_tensor_tensor(lnst[:, 8:10], lnst[:, 0:2], 1.0 / 64,
                                                        lnst[:, 6:8], ALU.mult, ALU.subtract),
                 reads=(gb("lnst"),), writes=(gb("lnst"),), small=True, tag=f"V_var_{tagn}")
            S.op("ac", lambda e: e.activation(lnst[:, 10:12], lnst[:, 8:10], AF.Sqrt),
                 reads=(gb("lnst"),), writes=(gb("lnst"),), small=True, tag=f"E_sqrt_{tagn}")
            S.op("dv", lambda e: e.reciprocal(lnst[:, 12:14], lnst[:, 10:12]),
                 reads=(gb("lnst"),), writes=(gb("lnst"),), small=True, tag=f"V_rstd_{tagn}")
            for hg in range(2):
                S.op("dv", lambda e, hg=hg: e.tensor_scalar(
                        xh[:, hg * 64:(hg + 1) * 64], z2t[:, hg * 64:(hg + 1) * 64],
                        lnst[:, 4 + hg:5 + hg], lnst[:, 12 + hg:13 + hg],
                        ALU.subtract, ALU.mult),
                     reads=(gb("z2t"), gb("lnst")), writes=(gb("xh"),), tag=f"V6{hg}_{tagn}")
            S.op("gp", lambda e: e.tensor_mul(t2g[:, :], tgT8[:, :], gB[:, :]),
                 reads=(gb("tgT8"), gb("gB")), writes=(gb("t2g"),), tag=f"S4_{tagn}")
            S.op("gp", lambda e: e.tensor_mul(u1b[:, :], xh[:, :], g2B[:, :]),
                 reads=(gb("xh"), gb("g2B")), writes=(gb("u1b"),), tag=f"S5_{tagn}")
            S.op("dv", lambda e: e.scalar_tensor_tensor(gop[:, :], t2g[:, :], -1.0,
                                                        u1b[:, :], ALU.mult, ALU.add),
                 reads=(gb("t2g"), gb("u1b")), writes=(gb("gop"),), tag=f"V7a_{tagn}")
            S.op("dv", lambda e: e.tensor_add(gop[:, :], gop[:, :], gbB[:, :]),
                 reads=(gb("gop"), gb("gbB")), writes=(gb("gop"),), tag=f"V7b_{tagn}")
            S.op("dv", lambda e: e.tensor_mul(scr[:, 0:128], gop[:, :], xh[:, :]),
                 reads=(gb("gop"), gb("xh")), writes=(gb("scr"),), tag=f"V8a_{tagn}")
            S.op("dv", lambda e: e.tensor_reduce(lnst[:, 14:16],
                    scr[:, 0:128].rearrange("p (hg d) -> p hg d", d=64), AX.X, ALU.add),
                 reads=(gb("scr"),), writes=(gb("lnst"),), small=True, tag=f"V8b_{tagn}")
            S.op("dv", lambda e: e.tensor_reduce(lnst[:, 0:2],
                    gop[:, :].rearrange("p (hg d) -> p hg d", d=64), AX.X, ALU.add),
                 reads=(gb("gop"),), writes=(gb("lnst"),), small=True, tag=f"V9_{tagn}")
            S.op("dv", lambda e: e.tensor_scalar_mul(lnst[:, 2:4], lnst[:, 14:16], 1.0 / 64),
                 reads=(gb("lnst"),), writes=(gb("lnst"),), small=True, tag=f"V_mgx_{tagn}")
            S.op("dv", lambda e: e.tensor_scalar_mul(lnst[:, 4:6], lnst[:, 0:2], 1.0 / 64),
                 reads=(gb("lnst"),), writes=(gb("lnst"),), small=True, tag=f"V_mg_{tagn}")
            for hg in range(2):
                S.op("dv", lambda e, hg=hg: e.scalar_tensor_tensor(
                        ldx[:, hg * 64:(hg + 1) * 64], xh[:, hg * 64:(hg + 1) * 64],
                        lnst[:, 2 + hg:3 + hg], gop[:, hg * 64:(hg + 1) * 64],
                        ALU.mult, ALU.subtract),
                     reads=(gb("gop"), gb("xh"), gb("lnst")), writes=(gb("ldx"),),
                     tag=f"V10a{hg}_{tagn}")
                S.op("dv", lambda e, hg=hg: e.tensor_scalar(
                        ldx[:, hg * 64:(hg + 1) * 64], ldx[:, hg * 64:(hg + 1) * 64],
                        -1.0, lnst[:, 4 + hg:5 + hg], ALU.mult, ALU.subtract),
                     reads=(gb("ldx"), gb("lnst")), writes=(gb("ldx"),),
                     tag=f"V10b{hg}_{tagn}")
            for hg in range(2):
                for half, m, mb in ((0, mE, "c_mE"), (1, mO, "c_mO")):
                    S.op("dv", lambda e, hg=hg, half=half, m=m: e.scalar_tensor_tensor(
                            gst[:, hg * 128 + half * 64:hg * 128 + half * 64 + 64],
                            ldx[:, hg * 64:(hg + 1) * 64], lnst[:, 12 + hg:13 + hg],
                            m[:, :], ALU.mult, ALU.mult),
                         reads=(gb("ldx"), gb("lnst"), gb(mb)), writes=(gb("gst"),),
                         tag=f"V11{hg}{half}_{tagn}")
            for hg in range(2):
                S.op("dv", lambda e, hg=hg, n=n: e.tensor_scalar_mul(
                        gstp[:, hg * 128:(hg + 1) * 128], gst[:, hg * 128:(hg + 1) * 128],
                        lrf[:, hg * 32 + n:hg * 32 + n + 1]),
                     reads=(gb("gst"), gb("lrf")), writes=(gb("gstp"),), tag=f"V11p{hg}_{tagn}")
            for hg in range(2):
                S.op("pe", lambda e, hg=hg: e.transpose(
                        pbf[:, 384 + hg * 128:384 + hg * 128 + 128],
                        gst[:, hg * 128:(hg + 1) * 128], ID[:, :]),
                     reads=(gb("gst"), gb("c_identb")), writes=(gb("pbf"),), tag=f"T3_{hg}_{tagn}")
            S.op("ac", lambda e: e.activation(gbd[:, :], pbf[:, 384:640], AF.Copy),
                 reads=(gb("pbf"),), writes=(gb("gbd"),), tag=f"E7_{tagn}")
            for g in range(8):
                hg, s_ = g // 4, g % 4
                S.op("pe", lambda e, g=g, hg=hg, s_=s_: e.matmul(
                        pV[32 * s_:32 * s_ + 32, hg * 256:hg * 256 + 256],
                        gbd[:, hg * 128 + 32 * s_:hg * 128 + 32 * s_ + 32],
                        sw2t[:, g * 256:(g + 1) * 256],
                        start=True, stop=True, **tpos(0, 32 * s_)),
                     reads=(gb("gbd"), gb("sw2t")), writes=(gb("pV"),), tag=f"M3_{g}_{tagn}")
            S.op("dv", lambda e: e.scalar_tensor_tensor(scr[:, :], sg1[:, :], -1.0,
                                                        sg1[:, :], ALU.add, ALU.mult),
                 reads=(gb("sg1"),), writes=(gb("scr"),), tag=f"V12_{tagn}")
            S.op("gp", lambda e: e.tensor_scalar(sq1[:, :], sq1[:, :], GB1, GB2,
                                                 ALU.mult, ALU.add),
                 reads=(gb("sq1"),), writes=(gb("sq1"),), tag=f"S6_{tagn}")
            S.op("gp", lambda e: e.tensor_mul(scr[:, :], scr[:, :], sq1[:, :]),
                 reads=(gb("scr"), gb("sq1")), writes=(gb("scr"),), tag=f"S7_{tagn}")
            S.op("gp", lambda e: e.tensor_mul(scr[:, :], scr[:, :], z1t[:, :]),
                 reads=(gb("scr"), gb("z1t")), writes=(gb("scr"),), tag=f"V13_{tagn}")
            S.op("dv", lambda e: e.tensor_add(scr[:, :], scr[:, :], sg1[:, :]),
                 reads=(gb("scr"), gb("sg1")), writes=(gb("scr"),), tag=f"V14_{tagn}")
            for hg in range(2):
                S.op("dv", lambda e, hg=hg, n=n: e.scalar_tensor_tensor(
                        gz1p[:, hg * 256:(hg + 1) * 256], pV[:, hg * 256:(hg + 1) * 256],
                        lrf[:, hg * 32 + n:hg * 32 + n + 1], scr[:, hg * 256:(hg + 1) * 256],
                        ALU.mult, ALU.mult),
                     reads=(gb("pV"), gb("lrf"), gb("scr")), writes=(gb("gz1p"),),
                     tag=f"V15{hg}_{tagn}")
            for g in range(8):
                hg, s_ = g // 4, g % 4
                S.op("pe", lambda e, g=g, hg=hg, s_=s_: e.matmul(
                        pz1[32 * s_:32 * s_ + 32, hg * 256:hg * 256 + 256],
                        m5s[32 * s_:32 * s_ + 32, hg * 32:hg * 32 + 32],
                        gz1p[32 * s_:32 * s_ + 32, hg * 256:hg * 256 + 256],
                        start=True, stop=True, **tpos(32 * s_, 32 * s_)),
                     reads=(gb("m5s"), gb("gz1p")), writes=(gb("pz1"),), tag=f"M5_{g}_{tagn}")
            S.op("dv", lambda e: e.tensor_add(z1bt[:, :], pz1b[:, :], b1b[:, :]),
                 reads=(gb("pz1b"), gb("b1b")), writes=(gb("z1bt"),), tag=f"V18a_{tagn}")
            S.op("dv", lambda e: e.tensor_add(z1bt[:, :], z1bt[:, :], pz1[:, :]),
                 reads=(gb("z1bt"), gb("pz1")), writes=(gb("z1bt"),), tag=f"V18b_{tagn}")
            for g in range(8):
                hg, s_ = g // 4, g % 4
                slot = g % 3
                S.op("pe", lambda e, g=g, hg=hg, s_=s_, slot=slot: e.matmul(
                        PDWT[slot][0:128, 0:256],
                        kst[32 * s_:32 * s_ + 32, hg * 128:(hg + 1) * 128],
                        gz1p[32 * s_:32 * s_ + 32, hg * 256:(hg + 1) * 256],
                        start=True, stop=True, **tpos(32 * s_, 0)),
                     reads=(gb("kst"), gb("gz1p")), writes=(PDW[slot],), tag=f"MD1_{g}_{tagn}")
                S.op("dv", lambda e, g=g, slot=slot: e.scalar_tensor_tensor(
                        sw1[:, g * 256:(g + 1) * 256], PDWT[slot][0:128, 0:256],
                        nt15[:, 0:1], sw1[:, g * 256:(g + 1) * 256], ALU.mult, ALU.add),
                     reads=(PDW[slot], gb("nt15"), gb("sw1")), writes=(gb("sw1"),),
                     tag=f"VU1_{g}_{tagn}")
            for g in range(8):
                hg, s_ = g // 4, g % 4
                slot = g % 3
                S.op("pe", lambda e, g=g, hg=hg, s_=s_, slot=slot: e.matmul(
                        PDWT[slot][0:128, 0:256],
                        gstp[32 * s_:32 * s_ + 32, hg * 128:(hg + 1) * 128],
                        x2[32 * s_:32 * s_ + 32, hg * 256:(hg + 1) * 256],
                        start=True, stop=True, **tpos(32 * s_, 0)),
                     reads=(gb("gstp"), gb("x2")), writes=(PDW[slot],), tag=f"MD2_{g}_{tagn}")
                S.op("dv", lambda e, g=g, slot=slot: e.scalar_tensor_tensor(
                        sw2t[:, g * 256:(g + 1) * 256], PDWT[slot][0:128, 0:256],
                        nt15[:, 0:1], sw2t[:, g * 256:(g + 1) * 256], ALU.mult, ALU.add),
                     reads=(PDW[slot], gb("nt15"), gb("sw2t")), writes=(gb("sw2t"),),
                     tag=f"VU2_{g}_{tagn}")
            for hg in range(2):
                S.op("pe", lambda e, hg=hg: e.matmul(pd0[0:16, 256:512], pones[:, :],
                                                     gz1p[:, hg * 256:(hg + 1) * 256],
                                                     start=True, stop=True),
                     reads=(gb("gz1p"), gb("c_pones")), writes=(gb("pd0"),),
                     tag=f"MB1{hg}_{tagn}")
                S.op("ac", lambda e, hg=hg: e.activation(bsum[:, hg * 256:(hg + 1) * 256],
                                                         pd0[0:16, 256:512], AF.Copy),
                     reads=(gb("pd0"),), writes=(gb("bsum"),), tag=f"EB1{hg}_{tagn}")
                S.op("pe", lambda e, hg=hg: e.matmul(pd1[0:128, 0:256], rep16[:, :],
                                                     bsum[0:16, hg * 256:(hg + 1) * 256],
                                                     start=True, stop=True),
                     reads=(gb("bsum"), gb("c_rep16")), writes=(gb("pd1"),),
                     tag=f"MBB1{hg}_{tagn}")
                S.op("dv", lambda e, hg=hg: e.scalar_tensor_tensor(
                        b1b[:, hg * 256:(hg + 1) * 256], pd1[0:128, 0:256], nt15[:, 0:1],
                        b1b[:, hg * 256:(hg + 1) * 256], ALU.mult, ALU.add),
                     reads=(gb("pd1"), gb("nt15"), gb("b1b")), writes=(gb("b1b"),),
                     tag=f"VB1{hg}_{tagn}")
            S.op("ac", lambda e: e.activation(scr[:, :], z1bt[:, :], AF.Square),
                 reads=(gb("z1bt"),), writes=(gb("scr"),), tag=f"E9_{tagn}")
            S.op("gp", lambda e: e.tensor_scalar(scr[:, :], scr[:, :], GC1, GC2,
                                                 ALU.mult, ALU.add),
                 reads=(gb("scr"),), writes=(gb("scr"),), tag=f"S8_{tagn}")
            S.op("gp", lambda e: e.tensor_mul(scr[:, :], scr[:, :], z1bt[:, :]),
                 reads=(gb("scr"), gb("z1bt")), writes=(gb("scr"),), tag=f"S9_{tagn}")
            S.op("ac", lambda e: e.activation(scr[:, :], scr[:, :], AF.Sigmoid),
                 reads=(gb("scr"),), writes=(gb("scr"),), tag=f"E10_{tagn}")
            S.op("dv", lambda e: e.tensor_mul(x2b[:, :], scr[:, :], z1bt[:, :]),
                 reads=(gb("scr"), gb("z1bt")), writes=(gb("x2b"),), tag=f"V19_{tagn}")
            for i in range(4):
                hg, fc = i // 2, i % 2
                sl = 640 + (i % 2) * 128
                S.op("pe", lambda e, hg=hg, fc=fc, sl=sl: e.transpose(
                        pbf[:, sl:sl + 128],
                        x2b[:, hg * 256 + fc * 128:hg * 256 + fc * 128 + 128], ID[:, :]),
                     reads=(gb("x2b"), gb("c_identb")), writes=(gb("pbf"),), tag=f"T4_{i}_{tagn}")
                S.op("ac", lambda e, hg=hg, fc=fc, sl=sl: e.activation(
                        x2bf[:, fc * 256 + hg * 128:fc * 256 + hg * 128 + 128],
                        pbf[:, sl:sl + 128], AF.Copy),
                     reads=(gb("pbf"),), writes=(gb("x2bf"),), tag=f"E11_{i}_{tagn}")
            for g in range(8):
                hg, s_ = g // 4, g % 4
                for fc in range(2):
                    S.op("pe", lambda e, g=g, hg=hg, s_=s_, fc=fc: e.matmul(
                            pmx[32 * s_:32 * s_ + 32, 320 + hg * 32:320 + hg * 32 + 32],
                            x2f[:, fc * 256 + hg * 128 + 32 * s_:fc * 256 + hg * 128 + 32 * s_ + 32],
                            x2bf[:, fc * 256 + hg * 128 + 32 * s_:fc * 256 + hg * 128 + 32 * s_ + 32],
                            start=(fc == 0), stop=(fc == 1), **tpos(0, 32 * s_)),
                         reads=(gb("x2f"), gb("x2bf")), writes=(gb("pmx"),),
                         tag=f"MA2_{g}_{fc}_{tagn}")
            S.op("dv", lambda e: e.scalar_tensor_tensor(m7s[:, :], pmx[0:128, 320:384], 1.0,
                                                        nmsb[:, :], ALU.add, ALU.mult),
                 reads=(gb("pmx"), gb("nmsb")), writes=(gb("m7s"),), tag=f"M7b_{tagn}")
            for g in range(8):
                hg, s_ = g // 4, g % 4
                for fc in range(2):
                    S.op("pe", lambda e, g=g, hg=hg, s_=s_, fc=fc: e.matmul(
                            pmx[32 * s_:32 * s_ + 32, hg * 128:hg * 128 + 128],
                            x2bf[:, fc * 256 + hg * 128 + 32 * s_:fc * 256 + hg * 128 + 32 * s_ + 32],
                            w2c[:, g * 256 + fc * 128:g * 256 + fc * 128 + 128],
                            start=(fc == 0), stop=False, **tpos(0, 32 * s_)),
                         reads=(gb("x2bf"), gb("w2c")), writes=(gb("pmx"),),
                         tag=f"M8_{g}_{fc}_{tagn}")
                S.op("pe", lambda e, g=g, hg=hg, s_=s_: e.matmul(
                        pmx[32 * s_:32 * s_ + 32, hg * 128:hg * 128 + 128],
                        m7s[32 * s_:32 * s_ + 32, hg * 32:hg * 32 + 32],
                        gstp[32 * s_:32 * s_ + 32, hg * 128:(hg + 1) * 128],
                        start=False, stop=True, skip_group_check=True,
                        **tpos(32 * s_, 32 * s_)),
                     reads=(gb("m7s"), gb("gstp")), writes=(gb("pmx"),), tag=f"M7_{g}_{tagn}")
            for g in range(8):
                hg, s_ = g // 4, g % 4
                slot = g % 3
                for fc in range(2):
                    S.op("pe", lambda e, g=g, hg=hg, s_=s_, fc=fc, slot=slot: e.matmul(
                            PDWT[slot][0:128, fc * 128:fc * 128 + 128],
                            x2[32 * s_:32 * s_ + 32, hg * 256 + fc * 128:hg * 256 + fc * 128 + 128],
                            gstp[32 * s_:32 * s_ + 32, hg * 128:(hg + 1) * 128],
                            start=True, stop=True, **tpos(32 * s_, 0)),
                         reads=(gb("x2"), gb("gstp")), writes=(PDW[slot],),
                         tag=f"MD3_{g}_{fc}_{tagn}")
                S.op("dv", lambda e, g=g, slot=slot: e.scalar_tensor_tensor(
                        w2c[:, g * 256:(g + 1) * 256], PDWT[slot][0:128, 0:256],
                        nt15[:, 0:1], w2c[:, g * 256:(g + 1) * 256], ALU.mult, ALU.add),
                     reads=(PDW[slot], gb("nt15"), gb("w2c")), writes=(gb("w2c"),),
                     tag=f"VU3_{g}_{tagn}")
            S.op("ac", lambda e: e.activation(z2s[:, :], pmx[:, 0:256], AF.Copy),
                 reads=(gb("pmx"),), writes=(gb("z2s"),), tag=f"E5b_{tagn}")
            for hg in range(2):
                S.op("dv", lambda e, hg=hg: e.select(
                        z2bt[:, hg * 64:(hg + 1) * 64], selu8[:, :],
                        z2s[:, hg * 128:hg * 128 + 64], z2s[:, hg * 128 + 64:hg * 128 + 128]),
                     reads=(gb("z2s"), gb("c_selu8")), writes=(gb("z2bt"),),
                     tag=f"V22a{hg}_{tagn}")
            S.op("dv", lambda e: e.tensor_add(z2bt[:, :], z2bt[:, :], b2b[:, :]),
                 reads=(gb("z2bt"), gb("b2b")), writes=(gb("z2bt"),), tag=f"V22b_{tagn}")
            for hg in range(2):
                S.op("pe", lambda e, hg=hg: e.matmul(pd0[0:16, 256:384], pones[:, :],
                                                     gstp[:, hg * 128:(hg + 1) * 128],
                                                     start=True, stop=True),
                     reads=(gb("gstp"), gb("c_pones")), writes=(gb("pd0"),),
                     tag=f"MB2{hg}_{tagn}")
                S.op("ac", lambda e, hg=hg: e.activation(bsum[:, hg * 128:hg * 128 + 128],
                                                         pd0[0:16, 256:384], AF.Copy),
                     reads=(gb("pd0"),), writes=(gb("bsum2"),), tag=f"EB2{hg}_{tagn}")
                S.op("pe", lambda e, hg=hg: e.matmul(pd2[0:128, 0:128], rep16[:, :],
                                                     bsum[0:16, hg * 128:hg * 128 + 128],
                                                     start=True, stop=True),
                     reads=(gb("bsum2"), gb("c_rep16")), writes=(gb("pd2"),),
                     tag=f"MBB2{hg}_{tagn}")
                S.op("ac", lambda e: e.activation(bde[:, :], pd2[0:128, 0:128], AF.Copy),
                     reads=(gb("pd2"),), writes=(gb("bde"),), tag=f"EBB2{hg}_{tagn}")
                S.op("dv", lambda e: e.select(bde[:, 0:64], selu8[:, :], bde[:, 0:64],
                                              bde[:, 64:128]),
                     reads=(gb("bde"), gb("c_selu8")), writes=(gb("bde"),),
                     tag=f"VBS{hg}_{tagn}")
                S.op("dv", lambda e, hg=hg: e.scalar_tensor_tensor(
                        b2b[:, hg * 64:(hg + 1) * 64], bde[:, 0:64], nt15[:, 0:1],
                        b2b[:, hg * 64:(hg + 1) * 64], ALU.mult, ALU.add),
                     reads=(gb("bde"), gb("nt15"), gb("b2b")), writes=(gb("b2b"),),
                     tag=f"VB2{hg}_{tagn}")
            S.op("gp", lambda e: e.tensor_mul(scr[:, 0:128], z2bt[:, :], z2bt[:, :]),
                 reads=(gb("z2bt"),), writes=(gb("scr"),), tag=f"S_sq2_{tagn}")
            S.op("dv", lambda e: e.tensor_reduce(lnst[:, 0:2],
                    scr[:, 0:128].rearrange("p (hg d) -> p hg d", d=64), AX.X, ALU.add),
                 reads=(gb("scr"),), writes=(gb("lnst"),), small=True, tag=f"V_ssq2_{tagn}")
            S.op("dv", lambda e: e.tensor_reduce(lnst[:, 2:4],
                    z2bt[:, :].rearrange("p (hg d) -> p hg d", d=64), AX.X, ALU.add),
                 reads=(gb("z2bt"),), writes=(gb("lnst"),), small=True, tag=f"V_s12_{tagn}")
            S.op("dv", lambda e: e.tensor_scalar_mul(lnst[:, 4:6], lnst[:, 2:4], 1.0 / 64),
                 reads=(gb("lnst"),), writes=(gb("lnst"),), small=True, tag=f"V_mu2_{tagn}")
            S.op("dv", lambda e: e.tensor_tensor(lnst[:, 6:8], lnst[:, 4:6], lnst[:, 4:6],
                                                 ALU.mult),
                 reads=(gb("lnst"),), writes=(gb("lnst"),), small=True, tag=f"V_m22_{tagn}")
            S.op("dv", lambda e: e.tensor_scalar(lnst[:, 6:8], lnst[:, 6:8], 1.0, -EPS_TTT,
                                                 ALU.mult, ALU.add),
                 reads=(gb("lnst"),), writes=(gb("lnst"),), small=True, tag=f"V_m2e2_{tagn}")
            S.op("dv", lambda e: e.scalar_tensor_tensor(lnst[:, 8:10], lnst[:, 0:2], 1.0 / 64,
                                                        lnst[:, 6:8], ALU.mult, ALU.subtract),
                 reads=(gb("lnst"),), writes=(gb("lnst"),), small=True, tag=f"V_var2_{tagn}")
            S.op("ac", lambda e: e.activation(lnst[:, 10:12], lnst[:, 8:10], AF.Sqrt),
                 reads=(gb("lnst"),), writes=(gb("lnst"),), small=True, tag=f"E_sqrt2_{tagn}")
            S.op("dv", lambda e: e.reciprocal(lnst[:, 12:14], lnst[:, 10:12]),
                 reads=(gb("lnst"),), writes=(gb("lnst"),), small=True, tag=f"V_rstd2_{tagn}")
            for hg in range(2):
                S.op("dv", lambda e, hg=hg: e.tensor_scalar(
                        xh2[:, hg * 64:(hg + 1) * 64], z2bt[:, hg * 64:(hg + 1) * 64],
                        lnst[:, 4 + hg:5 + hg], lnst[:, 12 + hg:13 + hg],
                        ALU.subtract, ALU.mult),
                     reads=(gb("z2bt"), gb("lnst")), writes=(gb("xh2"),), tag=f"V27{hg}_{tagn}")
            S.op("gp", lambda e: e.tensor_mul(xh2[:, :], xh2[:, :], gB[:, :]),
                 reads=(gb("xh2"), gb("gB")), writes=(gb("xh2"),), tag=f"V28_{tagn}")
            S.op("gp", lambda e: e.tensor_add(xh2[:, :], xh2[:, :], bB[:, :]),
                 reads=(gb("xh2"), gb("bB")), writes=(gb("xh2"),), tag=f"V29a_{tagn}")
            S.op("dv", lambda e, n=n: e.tensor_add(
                    yblk[:, n * 128:(n + 1) * 128], xh2[:, :], xqT8[:, :]),
                 reads=(gb("xh2"), gb("xqT8")), writes=(gb("yblk"),), tag=f"V29_{tagn}")
        if debug and bi == nblk - 1:
            dbg_list = [("z1t", z1t, 512), ("x2", x2, 512), ("x2f", x2f, 512),
                        ("z2t", z2t, 128), ("lnst", lnst, 16), ("xh", xh, 128),
                        ("gop", gop, 128), ("ldx", ldx, 128), ("gst", gst, 256),
                        ("gstp", gstp, 256), ("gbd", gbd, 256), ("m5s", m5s, 64),
                        ("gz1p", gz1p, 512), ("z1bt", z1bt, 512), ("x2b", x2b, 512),
                        ("m7s", m7s, 64), ("z2bt", z2bt, 128), ("kst", kst, 256),
                        ("lrf", lrf, 64), ("sw1", sw1, 2048), ("sw2t", sw2t, 2048),
                        ("w2c", w2c, 2048), ("b1b", b1b, 512), ("xqT8", xqT8, 128),
                        ("xkT8", xkT8, 128), ("tgT8", tgT8, 128), ("nmsb", nmsb, 64)]
            off = 0
            S.dbg_offsets = {}
            for nm, tb, w in dbg_list:
                bn = {"sw1": "sw1", "sw2t": "sw2t", "w2c": "w2c", "b1b": "b1b"}.get(nm, nm)
                S.op("gd", lambda e, tb=tb, off=off, w=w: e.dma_start(
                        out=ddbg[0:tb.shape[0], off:off + w], in_=tb[:, 0:w]),
                     reads=(gb(bn),), writes=(gb("dbgst"),), tag=f"dbg_{nm}")
                S.dbg_offsets[nm] = (off, w)
                off += w
        for h8 in range(8):
            for hg_ in range(2):
                S.op("gd", lambda e, bi=bi, h8=h8, hg_=hg_: e.dma_start(
                        out=dy.rearrange("(nb n t) (hg h8 d) -> nb h8 hg t n d",
                                         nb=NBLK, t=16, hg=2, d=64)[bi, h8, hg_],
                        in_=yblk[h8 * 16:(h8 + 1) * 16, :]
                            .rearrange("p (n hg d) -> p hg n d", hg=2, d=64)[:, hg_]),
                     reads=(gb("yblk"),), writes=(gb("ystore"),), tag=f"yst{bi}_{h8}_{hg_}")
    return (nc, S) if standalone else None


def _prep_l3_inputs(inputs, r2out):
    cst = l3_consts()
    hc = host_constants()
    W1 = inputs["L0_attn_W1"].astype(np.float32)
    W2 = inputs["L0_attn_W2"].astype(np.float32)
    b1 = inputs["L0_attn_b1"].astype(np.float32)
    b2 = inputs["L0_attn_b2"].astype(np.float32)
    lnw = inputs["L0_attn_ttt_norm_weight"].astype(np.float32)
    lnb = inputs["L0_attn_ttt_norm_bias"].astype(np.float32)
    d = {
        "xq_fm": r2out["xq_fm"], "xk_fm": r2out["xk_fm"], "xv_fm": r2out["xv_fm"],
        "lr_fm": r2out["lr_fm"],
        "w1p": np.ascontiguousarray(W1.reshape(8, 128, 256)),
        "w2tp": np.ascontiguousarray(W2.transpose(0, 2, 1).reshape(8, 128, 256)),
        "w2cp": np.ascontiguousarray(W2.reshape(8, 2, 2, 128, 64)
                                     .transpose(0, 2, 3, 1, 4).reshape(8, 2, 128, 128)),
        "b1b0": np.ascontiguousarray(np.repeat(b1.reshape(16, 256), 16, axis=0)
                                     .reshape(2, 128, 256)),
        "b2b0": np.ascontiguousarray(np.repeat(b2.reshape(16, 64), 16, axis=0)
                                     .reshape(2, 128, 64)),
        "lnw8": np.ascontiguousarray(np.repeat(lnw, 16, axis=0).reshape(2, 128, 64)),
        "lnb8": np.ascontiguousarray(np.repeat(lnb, 16, axis=0).reshape(2, 128, 64)),
        "lti": inputs["L0_attn_learnable_token_idx"].astype(np.float32).reshape(1, 16),
        "identb": hc["ident"],
    }
    for k_ in ("id32", "selh", "dpat", "pones", "rep16", "mE", "mO", "selu8",
               "negmb2", "invk", "onesb"):
        d[k_] = cst[k_]
    return d



def _stats_ln(S, gb, eng_s, eng_v, xtile, stx, base, tagp, eps, rd_buf, st_buf, sq_scr,
              sq_buf):
    S.op("ac", lambda e: e.activation(sq_scr[:, :], xtile, AF.Square,
                                      accum_out=stx[:, base + 0:base + 1]),
         reads=(rd_buf,), writes=(st_buf, sq_buf), tag=f"sq_{tagp}")
    S.op("dv", lambda e: e.reduce_sum(stx[:, base + 4:base + 5], xtile, AX.X),
         reads=(rd_buf,), writes=(st_buf,), tag=f"s1_{tagp}")
    S.op("dv", lambda e: e.tensor_scalar_mul(stx[:, base + 1:base + 2],
                                             stx[:, base + 4:base + 5], 1.0 / C),
         reads=(st_buf,), writes=(st_buf,), tag=f"mu_{tagp}")
    S.op("dv", lambda e: e.tensor_scalar(stx[:, base + 2:base + 3], stx[:, base + 1:base + 2],
                                         stx[:, base + 1:base + 2], -eps, ALU.mult, ALU.add),
         reads=(st_buf,), writes=(st_buf,), tag=f"m2_{tagp}")
    S.op("dv", lambda e: e.scalar_tensor_tensor(stx[:, base + 2:base + 3],
                                                stx[:, base + 0:base + 1], 1.0 / C,
                                                stx[:, base + 2:base + 3],
                                                ALU.mult, ALU.subtract),
         reads=(st_buf,), writes=(st_buf,), tag=f"vr_{tagp}")
    S.op("ac", lambda e: e.activation(stx[:, base + 2:base + 3], stx[:, base + 2:base + 3],
                                      AF.Sqrt),
         reads=(st_buf,), writes=(st_buf,), tag=f"sx_{tagp}")
    S.op("dv", lambda e: e.reciprocal(stx[:, base + 3:base + 4], stx[:, base + 2:base + 3]),
         reads=(st_buf,), writes=(st_buf,), tag=f"rc_{tagp}")


def build_l4(cx=None):
    standalone = cx is None
    if standalone:
        cx = Ctx(bass.Bass("TRN2", target_bir_lowering=False), False)
        cx.S = Sched()
        cx.phase("l4")
    nc = cx.nc
    dyt = cx.dram_inter("y_tm", [T, C], BF16, "in")
    dx = cx.dram_inter("x", [T, C], F32, "input")
    dxn = cx.dram_inter("xn_bf", [T, C], BF16, "in")
    dA = cx.dram_inter("a_bf", [C, C], BF16, "in")
    dwo = cx.dram_in("wo_t", [C, C], F32)
    dpnw = cx.dram_in("pnw_rep", [128, C], F32)
    dpnb = cx.dram_in("pnb_rep", [128, C], F32)
    dfb = cx.dram_in("fbias_rep", [128, C], F32)
    dident = cx.dram_in("identb", [128, 128], BF16)
    dx1 = cx.dram_inter("x1", [T, C], F32, "out")
    dxn1 = cx.dram_inter("xn1_bf", [T, C], BF16, "out")

    A = cx.sb
    xnT = A("xnT", [128, 8 * T], BF16)
    wo_s = A("wo_s", [128, 8 * C], BF16)
    A_s = A("A_s", [128, 8 * C], BF16)
    pnw = A("pnw", [128, C], F32)
    pnb = A("pnb", [128, C], F32)
    fbias = A("fbias", [128, C], F32)
    identb = A("identb_s", [128, 128], BF16)
    yb = A("yb", [128, 2 * C], BF16)
    xb = A("xb", [128, 2 * C], F32)
    ynb = A("ynb", [128, 2 * C], F32)
    ynbf = A("ynbf", [128, 2 * C], BF16)
    ynT = A("ynT", [128, 2 * C], BF16)
    g0b = A("g0b", [128, 2 * C], F32)
    sgb = A("sgb", [128, 2 * C], F32)
    x1b = A("x1b", [128, 2 * C], F32)
    xn1b = A("xn1b", [128, 2 * C], BF16)
    stx = A("stx", [128, 20], F32)
    sqs = A("sqs", [128, 2 * C], F32)

    P = cx.psum
    ps0 = P("ps0", [128, 512], F32)
    ps1 = P("ps1", [128, 512], F32)
    pg0 = P("pg0", [128, 512], F32)
    pg1 = P("pg1", [128, 512], F32)
    ptrA = P("ptrA", [128, 512], BF16)
    ptrB = P("ptrB", [128, 512], BF16)

    S = cx.S
    gb = cx.gb

    for oc in range(8):
        S.op("sd", lambda e, oc=oc: e.dma_start_transpose(
                xnT[:, oc * T:(oc + 1) * T], dxn[:, oc * 128:(oc + 1) * 128]),
             writes=(gb("xnT"),), tag=f"ld_xnT{oc}")
    S.op("gd", lambda e: e.dma_start(out=wo_s[:, :].rearrange("p (a o) -> p a o", o=C),
                                     in_=dwo.rearrange("(a p) o -> p a o", p=128)),
         writes=(gb("wo_s"),), tag="ld_wo")
    S.op("sd", lambda e: e.dma_start(out=A_s[:, :].rearrange("p (a o) -> p a o", o=C),
                                     in_=dA.rearrange("(a p) o -> p a o", p=128)),
         writes=(gb("A_s"),), tag="ld_A")
    for dst, src, tg in ((pnw, dpnw, "pnw"), (pnb, dpnb, "pnb"), (fbias, dfb, "fbias"),
                         (identb, dident, "identb")):
        S.op("sd", lambda e, d=dst, s=src: e.dma_start(out=d[:, :], in_=s[:, :]),
             writes=(gb(tg),), tag="ld_" + tg)

    for i in range(NT):
        p_ = i % 2
        sp = str(p_)
        b0 = p_ * C
        tg = f"t{i}"
        S.op("sd", lambda e, i=i, b0=b0: e.dma_start(out=yb[:, b0:b0 + C],
                                                     in_=dyt[i * 128:(i + 1) * 128, :]),
             writes=(gb("yb" + sp),), tag=f"ldy_{tg}")
        S.op("sd", lambda e, i=i, b0=b0: e.dma_start(out=xb[:, b0:b0 + C],
                                                     in_=dx[i * 128:(i + 1) * 128, :]),
             writes=(gb("xb" + sp),), tag=f"ldx_{tg}")
        _stats_ln(S, gb, None, None, yb[:, b0:b0 + C], stx, p_ * 5, tg, EPS_LN,
                  gb("yb" + sp), gb("stx" + sp), sqs[:, b0:b0 + C], gb("sqs" + sp))
        S.op("dv", lambda e, b0=b0, p_=p_: e.tensor_scalar(
                ynb[:, b0:b0 + C], yb[:, b0:b0 + C],
                stx[:, p_ * 5 + 1:p_ * 5 + 2], stx[:, p_ * 5 + 3:p_ * 5 + 4],
                ALU.subtract, ALU.mult),
             reads=(gb("yb" + sp), gb("stx" + sp)), writes=(gb("ynb" + sp),),
             tag=f"yn_{tg}")
        S.op("gp", lambda e, b0=b0: e.tensor_mul(ynb[:, b0:b0 + C], ynb[:, b0:b0 + C],
                                                 pnw[:, :]),
             reads=(gb("ynb" + sp), gb("pnw")), writes=(gb("ynb" + sp),), tag=f"ynw_{tg}")
        S.op("dv", lambda e, b0=b0: e.tensor_add(ynbf[:, b0:b0 + C], ynb[:, b0:b0 + C],
                                                 pnb[:, :]),
             reads=(gb("ynb" + sp), gb("pnb")), writes=(gb("ynbf" + sp),), tag=f"ynb_{tg}")
        for oc in range(8):
            ptt = ptrA if oc % 2 == 0 else ptrB
            ptb = gb("ptrA" if oc % 2 == 0 else "ptrB")
            sl = ((oc // 2) % 4) * 128
            S.op("pe", lambda e, oc=oc, sl=sl, b0=b0, ptt=ptt: e.transpose(
                    ptt[:, sl:sl + 128], ynbf[:, b0 + oc * 128:b0 + (oc + 1) * 128],
                    identb[:, :]),
                 reads=(gb("ynbf" + sp), gb("identb")), writes=(ptb,),
                 tag=f"Ty{oc}_{tg}")
            S.op("ac", lambda e, oc=oc, sl=sl, b0=b0, ptt=ptt: e.activation(
                    ynT[:, b0 + oc * 128:b0 + (oc + 1) * 128], ptt[:, sl:sl + 128],
                    AF.Copy),
                 reads=(ptb,), writes=(gb("ynT" + sp),), tag=f"Ey{oc}_{tg}")
        for half, pp, ppb in ((0, ps0, "ps0"), (1, ps1, "ps1")):
            for oc in range(8):
                S.op("pe", lambda e, oc=oc, half=half, pp=pp, b0=b0: e.matmul(
                        pp[:, 0:512], ynT[:, b0 + oc * 128:b0 + (oc + 1) * 128],
                        wo_s[:, oc * C + half * 512:oc * C + (half + 1) * 512],
                        start=(oc == 0), stop=(oc == 7)),
                     reads=(gb("ynT" + sp), gb("wo_s")), writes=(gb(ppb),),
                     tag=f"Ms{half}_{oc}_{tg}")
        for half, pp, ppb in ((0, pg0, "pg0"), (1, pg1, "pg1")):
            for oc in range(8):
                S.op("pe", lambda e, i=i, oc=oc, half=half, pp=pp: e.matmul(
                        pp[:, 0:512], xnT[:, oc * T + i * 128:oc * T + (i + 1) * 128],
                        A_s[:, oc * C + half * 512:oc * C + (half + 1) * 512],
                        start=(oc == 0), stop=(oc == 7)),
                     reads=(gb("xnT"), gb("A_s")), writes=(gb(ppb),),
                     tag=f"Mg{half}_{oc}_{tg}")
        for half, pp, ppb in ((0, pg0, "pg0"), (1, pg1, "pg1")):
            S.op("dv", lambda e, half=half, pp=pp, b0=b0: e.tensor_add(
                    g0b[:, b0 + half * 512:b0 + (half + 1) * 512], pp[:, 0:512],
                    fbias[:, half * 512:(half + 1) * 512]),
                 reads=(gb(ppb), gb("fbias")), writes=(gb("g0b" + sp),), tag=f"gb{half}_{tg}")
        S.op("ac", lambda e, b0=b0: e.activation(g0b[:, b0:b0 + C], g0b[:, b0:b0 + C],
                                                 AF.Silu),
             reads=(gb("g0b" + sp),), writes=(gb("g0b" + sp),), tag=f"sil_{tg}")
        for half, pp, ppb in ((0, ps0, "ps0"), (1, ps1, "ps1")):
            S.op("dv", lambda e, half=half, pp=pp, b0=b0: e.tensor_mul(
                    sgb[:, b0 + half * 512:b0 + (half + 1) * 512], pp[:, 0:512],
                    g0b[:, b0 + half * 512:b0 + (half + 1) * 512]),
                 reads=(gb(ppb), gb("g0b" + sp)), writes=(gb("sgb" + sp),),
                 tag=f"sg{half}_{tg}")
        S.op("gp", lambda e, b0=b0: e.tensor_add(x1b[:, b0:b0 + C], sgb[:, b0:b0 + C],
                                                 xb[:, b0:b0 + C]),
             reads=(gb("sgb" + sp), gb("xb" + sp)), writes=(gb("x1b" + sp),),
             tag=f"x1_{tg}")
        S.op("gd", lambda e, i=i, b0=b0: e.dma_start(out=dx1[i * 128:(i + 1) * 128, :],
                                                     in_=x1b[:, b0:b0 + C]),
             reads=(gb("x1b" + sp),), writes=(gb("st_x1"),), tag=f"stx1_{tg}")
        _stats_ln(S, gb, None, None, x1b[:, b0:b0 + C], stx, 10 + p_ * 5, tg + "b", EPS_LN,
                  gb("x1b" + sp), gb("stx" + sp), sqs[:, b0:b0 + C], gb("sqs" + sp))
        S.op("dv", lambda e, b0=b0, p_=p_: e.tensor_scalar(
                xn1b[:, b0:b0 + C], x1b[:, b0:b0 + C],
                stx[:, 11 + p_ * 5:12 + p_ * 5], stx[:, 13 + p_ * 5:14 + p_ * 5],
                ALU.subtract, ALU.mult),
             reads=(gb("x1b" + sp), gb("stx" + sp)), writes=(gb("xn1b" + sp),),
             tag=f"xn1_{tg}")
        S.op("gd", lambda e, i=i, b0=b0: e.dma_start(out=dxn1[i * 128:(i + 1) * 128, :],
                                                     in_=xn1b[:, b0:b0 + C]),
             reads=(gb("xn1b" + sp),), writes=(gb("st_xn1"),), tag=f"stxn1_{tg}")
    return (nc, S) if standalone else None


def _prep_l4_inputs(inputs, core, r13):
    hc = host_constants()
    return {
        "y_tm": r13["y_tm"],
        "x": np.ascontiguousarray(inputs["L0_attn_hidden_states"][core]),
        "xn_bf": r13["xn_bf"],
        "a_bf": r13["a_bf"],
        "wo_t": np.ascontiguousarray(inputs["L0_attn_wo"].T),
        "pnw_rep": np.ascontiguousarray(
            np.broadcast_to(inputs["L0_attn_post_norm_weight"][None, :], (128, C))).astype(np.float32),
        "pnb_rep": np.ascontiguousarray(
            np.broadcast_to(inputs["L0_attn_post_norm_bias"][None, :], (128, C))).astype(np.float32),
        "fbias_rep": np.ascontiguousarray(
            np.broadcast_to(inputs["L0_ffn_bias"][None, :], (128, C))).astype(np.float32),
        "identb": hc["ident"],
    }



def build_l5(cx=None):
    standalone = cx is None
    if standalone:
        cx = Ctx(bass.Bass("TRN2", target_bir_lowering=False), False)
        cx.S = Sched()
        cx.phase("l5")
    nc = cx.nc
    dxn1 = cx.dram_inter("xn1_bf", [T, C], BF16, "in")
    dw = {}
    for nm in ("maq_t", "mak_t", "mav_t", "ffq_t", "ffk_t", "ffv_t"):
        dw[nm] = cx.dram_in(nm, [C, C], F32)
    dn1 = cx.dram_in("n1w_col", [128, 8], F32)
    dqn = cx.dram_in("qn_rep", [128, C], F32)
    dkn = cx.dram_in("kn_rep", [128, C], F32)
    dident = cx.dram_in("identb", [128, 128], BF16)
    didf = cx.dram_in("identf", [128, 128], F32)
    douts = {}
    for nm in ("qma", "kma", "vma", "qff", "kff", "vff"):
        douts[nm] = cx.dram_inter(nm, [T, C], BF16, "out")
    dmstT = cx.dram_inter("mstT", [4, 16], F32, "out")

    A = cx.sb
    xn1T = A("xn1T", [128, 8 * T], BF16)
    w_s = [A(f"w{j}", [128, 8 * C], BF16) for j in range(3)]
    n1ws = A("n1ws", [128, 8], F32)
    qn_s = A("qn_s", [128, C], F32)
    kn_s = A("kn_s", [128, C], F32)
    identb = A("identb_s", [128, 128], BF16)
    identf = A("identf_s", [128, 128], F32)
    xn1b = A("xn1b", [128, C], BF16)
    stx = A("stx", [128, 16], F32)
    sqs = A("sqs", [128, C], F32)
    scr = A("scrq", [128, C], F32)
    s2h = A("s2h", [128, 16], F32)
    ob = A("ob", [128, 2 * C], BF16)
    macc = A("macc", [128, 4 * 16], F32)
    mrow = A("mrow", [16, 4], F32)
    P = cx.psum
    pj = [P(f"p{j}", [128, 512], F32) for j in range(4)]
    ptm = P("ptm", [128, 128], F32)

    S = cx.S
    gb = cx.gb

    for oc in range(8):
        S.op("sd", lambda e, oc=oc: e.dma_start_transpose(
                xn1T[:, oc * T:(oc + 1) * T], dxn1[:, oc * 128:(oc + 1) * 128]),
             writes=(gb("xn1T"),), tag=f"ld_xn1T{oc}")
    for dst, src, tg in ((n1ws, dn1, "n1ws"), (qn_s, dqn, "qn_s"), (kn_s, dkn, "kn_s"),
                         (identb, dident, "identb"), (identf, didf, "identf")):
        S.op("sd", lambda e, d=dst, s=src: e.dma_start(out=d[:, :], in_=s[:, :]),
             writes=(gb(tg),), tag="ld_" + tg)
    S.op("dv", lambda e: e.tensor_scalar_mul(qn_s[:, :], qn_s[:, :], 0.125),
         reads=(gb("qn_s"),), writes=(gb("qn_s"),), tag="qn_scale")
    S.op("dv", lambda e: e.memset(macc[:, :], 0.0), writes=(gb("macc"),), tag="z_macc")

    PH = [("maq_t", "mak_t", "mav_t"), ("ffq_t", "ffk_t", "ffv_t")]
    ONM = [("qma", "kma", "vma"), ("qff", "kff", "vff")]
    for ph in range(2):
        for j in range(3):
            S.op("gd", lambda e, ph=ph, j=j: e.dma_start(
                    out=w_s[j][:, :].rearrange("p (a o) -> p a o", o=C),
                    in_=dw[PH[ph][j]].rearrange("(a p) o -> p a o", p=128)),
                 writes=(gb(f"w{j}p{ph}"),), tag=f"ldw{ph}{j}")
            if ph == 1:
                for a in range(8):
                    S.op("dv", lambda e, j=j, a=a: e.tensor_scalar_mul(
                            w_s[j][:, a * C:(a + 1) * C], w_s[j][:, a * C:(a + 1) * C],
                            n1ws[:, a:a + 1]),
                         reads=(gb(f"w{j}p{ph}"), gb("n1ws")), writes=(gb(f"w{j}p{ph}"),),
                         tag=f"n1w{j}{a}")
        for i in range(NT):
            tg = f"p{ph}t{i}"
            if ph == 1:
                S.op("sd", lambda e, i=i: e.dma_start(out=xn1b[:, :],
                                                      in_=dxn1[i * 128:(i + 1) * 128, :]),
                     writes=(gb("xn1b"),), tag=f"ldxn1_{tg}")
                S.op("ac", lambda e: e.activation(sqs[:, :], xn1b[:, :], AF.Square,
                                                  accum_out=stx[:, 0:1]),
                     reads=(gb("xn1b"),), writes=(gb("stx"), gb("sqs")), tag=f"r1a_{tg}")
                S.op("dv", lambda e: e.tensor_scalar(stx[:, 1:2], stx[:, 0:1], 1.0 / C,
                                                     EPS_RMS, ALU.mult, ALU.add),
                     reads=(gb("stx"),), writes=(gb("stx"),), tag=f"r1b_{tg}")
                S.op("ac", lambda e: e.activation(stx[:, 1:2], stx[:, 1:2], AF.Sqrt),
                     reads=(gb("stx"),), writes=(gb("stx"),), tag=f"r1c_{tg}")
                S.op("dv", lambda e: e.reciprocal(stx[:, 2:3], stx[:, 1:2]),
                     reads=(gb("stx"),), writes=(gb("stx"),), tag=f"r1d_{tg}")
            for j in range(3):
                for half in range(2):
                    pp = pj[(j * 2 + half) % 4]
                    ppb = gb(f"pj{(j * 2 + half) % 4}")
                    for oc in range(8):
                        S.op("pe", lambda e, i=i, j=j, half=half, oc=oc, pp=pp: e.matmul(
                                pp[:, 0:512],
                                xn1T[:, oc * T + i * 128:oc * T + (i + 1) * 128],
                                w_s[j][:, oc * C + half * 512:oc * C + (half + 1) * 512],
                                start=(oc == 0), stop=(oc == 7)),
                             reads=(gb("xn1T"), gb(f"w{j}p{ph}")), writes=(ppb,),
                             tag=f"M{j}{half}_{oc}_{tg}")
                if ph == 0:
                    for half in range(2):
                        pp = pj[(j * 2 + half) % 4]
                        ppb = gb(f"pj{(j * 2 + half) % 4}")
                        sc = 0.125 if j == 0 else 1.0
                        S.op("ac", lambda e, j=j, half=half, pp=pp, sc=sc: e.activation(
                                ob[:, half * 512:(half + 1) * 512], pp[:, 0:512],
                                AF.Copy, scale=sc),
                             reads=(ppb,), writes=(gb("ob"),), tag=f"E{j}{half}_{tg}")
                else:
                    for half in range(2):
                        pp = pj[(j * 2 + half) % 4]
                        ppb = gb(f"pj{(j * 2 + half) % 4}")
                        S.op("dv", lambda e, half=half, pp=pp: e.tensor_scalar_mul(
                                scr[:, half * 512:(half + 1) * 512], pp[:, 0:512],
                                stx[:, 2:3]),
                             reads=(ppb, gb("stx")), writes=(gb("scrq"),),
                             tag=f"E{j}{half}_{tg}")
                    if j < 2:
                        S.op("gp", lambda e: e.tensor_mul(sqs[:, :], scr[:, :], scr[:, :]),
                             reads=(gb("scrq"),), writes=(gb("sqs"),), tag=f"hq1_{tg}_{j}")
                        S.op("dv", lambda e: e.tensor_reduce(
                                s2h[:, 0:16], sqs[:, :].rearrange("p (h d) -> p h d", d=64),
                                AX.X, ALU.add),
                             reads=(gb("sqs"),), writes=(gb("s2h"),), tag=f"hq2_{tg}_{j}")
                        S.op("dv", lambda e: e.tensor_scalar(s2h[:, 0:16], s2h[:, 0:16],
                                                             1.0 / 64, EPS_RMS,
                                                             ALU.mult, ALU.add),
                             reads=(gb("s2h"),), writes=(gb("s2h"),), tag=f"hq3_{tg}_{j}")
                        S.op("ac", lambda e: e.activation(s2h[:, 0:16], s2h[:, 0:16], AF.Sqrt),
                             reads=(gb("s2h"),), writes=(gb("s2h"),), tag=f"hq4_{tg}_{j}")
                        S.op("dv", lambda e: e.reciprocal(s2h[:, 16:32] if False else
                                                          s2h[:, 0:16], s2h[:, 0:16]),
                             reads=(gb("s2h"),), writes=(gb("s2h"),), tag=f"hq5_{tg}_{j}")
                        S.op("dv", lambda e: e.tensor_mul(
                                scr[:, :].rearrange("p (h d) -> p h d", d=64),
                                scr[:, :].rearrange("p (h d) -> p h d", d=64),
                                s2h[:, 0:16].rearrange("p (h d) -> p h d", d=1)
                                   .broadcast_to([128, 16, 64])),
                             reads=(gb("scrq"), gb("s2h")), writes=(gb("scrq"),),
                             tag=f"hq6_{tg}_{j}")
                        S.op("gp", lambda e, j=j: e.tensor_mul(
                                ob[:, 0:C], scr[:, :], qn_s[:, :] if j == 0 else kn_s[:, :]),
                             reads=(gb("scrq"), gb("qn_s") if j == 0 else gb("kn_s")),
                             writes=(gb("ob"),), tag=f"hq7_{tg}_{j}")
                    else:
                        S.op("dv", lambda e: e.tensor_copy(ob[:, 0:C], scr[:, :]),
                             reads=(gb("scrq"),), writes=(gb("ob"),), tag=f"E2c_{tg}")
                if j < 2:
                    mcol = ph * 2 + j
                    S.op("ac", lambda e: e.activation(sqs[:, :], ob[:, 0:C], AF.Square),
                         reads=(gb("ob"),), writes=(gb("sqs"),), tag=f"ms1_{tg}_{j}")
                    S.op("dv", lambda e: e.tensor_reduce(
                            s2h[:, 0:16], sqs[:, :].rearrange("p (h d) -> p h d", d=64),
                            AX.X, ALU.add),
                         reads=(gb("sqs"),), writes=(gb("s2h"),), tag=f"ms2_{tg}_{j}")
                    S.op("dv", lambda e, mcol=mcol: e.tensor_max(
                            macc[:, mcol * 16:(mcol + 1) * 16],
                            macc[:, mcol * 16:(mcol + 1) * 16], s2h[:, 0:16]),
                         reads=(gb("macc"), gb("s2h")), writes=(gb("macc"),),
                         tag=f"ms3_{tg}_{j}")
                S.op("gd", lambda e, ph=ph, j=j, i=i: e.dma_start(
                        out=douts[ONM[ph][j]][i * 128:(i + 1) * 128, :], in_=ob[:, 0:C]),
                     reads=(gb("ob"),), writes=(gb("st_o"),), tag=f"st_{tg}_{j}")
    for mcol in range(4):
        S.op("pe", lambda e, mcol=mcol: e.transpose(
                ptm[0:16, 0:128], macc[:, mcol * 16:(mcol + 1) * 16], identf[:, :]),
             reads=(gb("macc"), gb("identf")), writes=(gb("ptm"),), tag=f"Tm{mcol}")
        S.op("dv", lambda e, mcol=mcol: e.reduce_max(mrow[:, mcol:mcol + 1],
                                                     ptm[0:16, 0:128], AX.X),
             reads=(gb("ptm"),), writes=(gb("mrow"),), tag=f"Rm{mcol}")
    S.op("gd", lambda e: e.dma_start(out=dmstT.rearrange("a b -> b a"), in_=mrow[0:16, :],
                                     allow_slow_non_contiguous=True),
         reads=(gb("mrow"),), writes=(gb("st_m"),), tag="st_mst")
    return (nc, S) if standalone else None


def _prep_l5_inputs(inputs, xn1_bf):
    hc = host_constants()
    n1 = inputs["L1_ffn_n1_w"].astype(np.float32)
    return {
        "xn1_bf": xn1_bf,
        "maq_t": np.ascontiguousarray(inputs["L1_attn_Wq"].T),
        "mak_t": np.ascontiguousarray(inputs["L1_attn_Wk"].T),
        "mav_t": np.ascontiguousarray(inputs["L1_attn_Wv"].T),
        "ffq_t": np.ascontiguousarray(inputs["L1_ffn_wq"].T),
        "ffk_t": np.ascontiguousarray(inputs["L1_ffn_wk"].T),
        "ffv_t": np.ascontiguousarray(inputs["L1_ffn_wv"].T),
        "n1w_col": np.ascontiguousarray(n1.reshape(8, 128).T),
        "qn_rep": np.ascontiguousarray(
            np.broadcast_to(np.tile(inputs["L1_ffn_qn_w"].astype(np.float32), 16)[None, :],
                            (128, C))),
        "kn_rep": np.ascontiguousarray(
            np.broadcast_to(np.tile(inputs["L1_ffn_kn_w"].astype(np.float32), 16)[None, :],
                            (128, C))),
        "identb": hc["ident"],
        "identf": np.eye(128, dtype=np.float32),
    }



def _causal_masks():
    cm = np.zeros((128, 4 * 512), np.float32)
    for d in range(4):
        rel = d * 128
        for kr in range(128):
            cm[kr, d * 512 + kr + rel:(d + 1) * 512] = 1.0
    return cm.astype(BF)


def build_attn(kind, cx=None):
    standalone = cx is None
    if standalone:
        cx = Ctx(bass.Bass("TRN2", target_bir_lowering=False), False)
        cx.S = Sched()
        cx.phase("l6" if kind == "ffn" else "l8")
    nc = cx.nc
    dq = cx.dram_inter("qff" if kind == "ffn" else "qma", [T, C], BF16, "in")
    dk = cx.dram_inter("kff" if kind == "ffn" else "kma", [T, C], BF16, "in")
    dvv = cx.dram_inter("vff" if kind == "ffn" else "vma", [T, C], BF16, "in")
    dwo = cx.dram_in("wo_t", [C, C], F32)
    dmstT = cx.dram_inter("mstT", [4, 16], F32, "in")
    dmq = dmstT[2:3, :] if kind == "ffn" else dmstT[0:1, :]
    dmk = dmstT[3:4, :] if kind == "ffn" else dmstT[1:2, :]
    dcm = cx.dram_in("cmask", [128, 4 * 512], BF16)
    dones = cx.dram_in("onesb", [1, 128], BF16)
    if kind == "ffn":
        dxn1 = cx.dram_inter("xn1_bf", [T, C], BF16, "in")
        dn2 = cx.dram_in("n2_rep", [128, C], F32)
        dzm = cx.dram_in("zmask", [128, C], F32)
        dxi = cx.dram_inter("xi_bf", [T, C], BF16, "out")
        dxf = cx.dram_inter("xf_bf", [T, C], BF16, "out")
    else:
        dx1f = cx.dram_inter("x1", [T, C], F32, "in")
        dg1f = cx.dram_inter("g1_bf", [T, C], BF16, "in")
        dout = cx.dram_inter("outp", [T, C], F32, "final")

    A = cx.sb
    qT2 = A("qT2", [128, 2 * T], BF16)
    kT2 = A("kT2", [128, 2 * T], BF16)
    vaug = A("vaug", [128, 2 * 32 * 65], BF16)
    oall = A("oall", [128, 8 * T], BF16)
    pt = A("pt", [128, 2 * 512], BF16)
    cm_s = A("cm_s", [128, 4 * 512], BF16)
    ones_s = A("ones_s", [1, 128], BF16)
    mq_s = A("mq_s", [1, 16], F32)
    mk_s = A("mk_s", [1, 16], F32)
    negm = A("negm", [1, 16], F32)
    negmb_ = A("negmb_", [1, 16], BF16)
    nmc = A("nmc", [128, 1], F32)
    rl = A("rl", [1, 512], F32)
    rlb = A("rlb", [1, 512], BF16)
    rb = A("rb", [64, 512], F32)
    wo_s = A("wo_s", [128, 8 * C], BF16)
    aot = A("aot", [128, C], F32)
    xn1b = A("xn1b", [128, C], BF16)
    xib = A("xib", [128, C], F32)
    xibf = A("xibf", [128, C], BF16)
    xfb = A("xfb", [128, C], BF16)
    n2s = A("n2s", [128, C], F32)
    zms = A("zms", [128, C], F32)
    stx = A("stx", [128, 16], F32)
    sqs = A("sqs", [128, C], F32)
    ob = A("ob", [128, C], BF16)
    if kind == "main":
        x1b9 = A("x1b9", [128, 2 * C], F32)
        g1b9 = A("g1b9", [128, 2 * C], BF16)
        ob9 = A("ob9", [128, 2 * C], F32)

    P = cx.psum
    pS = [P("pS0", [128, 512], F32), P("pS1", [128, 512], F32),
          P("pS2", [128, 512], F32)]
    pO = [P("pO0", [128, 512], F32), P("pO1", [128, 512], F32)]
    pB = P("pB", [128, 512], F32)
    pW = [P("pW0", [128, 512], F32), P("pW1", [128, 512], F32)]

    S = cx.S
    gb = cx.gb

    def tpos2(row, col):
        if row or col:
            return {"tile_position": (row, col)}
        return {}

    S.op("gd", lambda e: e.dma_start(out=wo_s[:, :].rearrange("p (a o) -> p a o", o=C),
                                     in_=dwo.rearrange("(a p) o -> p a o", p=128)),
         writes=(gb("wo_s"),), tag="ld_wo")
    for dst, src, tg in ((cm_s, dcm, "cm"), (ones_s, dones, "ones"), (mq_s, dmq, "mq"),
                         (mk_s, dmk, "mk")):
        S.op("sd", lambda e, d=dst, s=src: e.dma_start(out=d[:, :], in_=s[:, :]),
             writes=(gb(tg),), tag="ld_" + tg)
    if kind == "ffn":
        for dst, src, tg in ((n2s, dn2, "n2s"), (zms, dzm, "zms")):
            S.op("sd", lambda e, d=dst, s=src: e.dma_start(out=d[:, :], in_=s[:, :]),
                 writes=(gb(tg),), tag="ld_" + tg)
    S.op("dv", lambda e: e.tensor_mul(negm[:, :], mq_s[:, :], mk_s[:, :]),
         reads=(gb("mq"), gb("mk")), writes=(gb("negm"),), tag="nm1")
    S.op("ac", lambda e: e.activation(negm[:, :], negm[:, :], AF.Sqrt),
         reads=(gb("negm"),), writes=(gb("negm"),), tag="nm2")
    S.op("dv", lambda e: e.tensor_scalar_mul(negmb_[:, :], negm[:, :], -1.0),
         reads=(gb("negm"),), writes=(gb("negmb"),), tag="nm3")
    S.op("dv", lambda e: e.memset(
            vaug[:, :].rearrange("p (b k w) -> p b k w", b=2, w=65)[:, :, :, 64:65], 1.0),
         writes=(gb("vones"),), tag="vones")

    for hp in range(8):
        pb_ = hp % 2
        S.op("sd", lambda e, hp=hp, pb_=pb_: e.dma_start_transpose(
                qT2[:, pb_ * T:(pb_ + 1) * T], dq[:, hp * 128:(hp + 1) * 128]),
             reads=(gb(f"qTfree{pb_}"),), writes=(gb(f"qT{pb_}"),), tag=f"ldq{hp}")
        S.op("sd", lambda e, hp=hp, pb_=pb_: e.dma_start_transpose(
                kT2[:, pb_ * T:(pb_ + 1) * T], dk[:, hp * 128:(hp + 1) * 128]),
             reads=(gb(f"kTfree{pb_}"),), writes=(gb(f"kT{pb_}"),), tag=f"ldk{hp}")
        for hh in range(2):
            h = 2 * hp + hh
            hb = h % 2
            hc = h // 2
            S.op("sd", lambda e, h=h, hb=hb: e.dma_start(
                    out=vaug[:, hb * 2080:(hb + 1) * 2080]
                        .rearrange("p (k w) -> p k w", w=65)[:, :, 0:64],
                    in_=dvv.rearrange("(k p) c -> p k c", p=128)[:, :, h * 64:(h + 1) * 64]),
                 reads=(gb(f"vfree{hb}"), gb("vones")), writes=(gb(f"v{hb}"),), tag=f"ldv{h}")
            S.op("pe", lambda e, h=h: e.matmul(pB[0:128, 0:1], ones_s[0:1, :],
                                               negmb_[0:1, h:h + 1],
                                               start=True, stop=True),
                 reads=(gb("negmb"), gb("ones")), writes=(gb("pB"),), tag=f"nmm{h}")
            S.op("ac", lambda e: e.activation(nmc[:, :], pB[0:128, 0:1], AF.Copy),
                 reads=(gb("pB"),), writes=(gb("nmc"),), tag=f"nme{h}")
            stages = [(qg, kt) for qg in range(8) for kt in range(4 * (qg + 1))]

            def emit_S(j, hh=hh, pb_=pb_):
                qg, kt = stages[j]
                ps = pS[j % 3]
                S.op("pe", lambda e, hh=hh, pb_=pb_, qg=qg, kt=kt, ps=ps: e.matmul(
                        ps[:, 0:512],
                        kT2[hh * 64:(hh + 1) * 64, pb_ * T + kt * 128:pb_ * T + kt * 128 + 128],
                        qT2[hh * 64:(hh + 1) * 64, pb_ * T + qg * 512:pb_ * T + (qg + 1) * 512],
                        start=True, stop=True, **(tpos2(hh * 64, 0))),
                     reads=(gb(f"kT{pb_}"), gb(f"qT{pb_}")), writes=(gb(f"pS{j % 3}"),),
                     tag=f"S_{h}_{qg}_{kt}")

            def emit_tail(qg, h=h, hc=hc, hb=hb):
                po = pO[qg % 2]
                pob = gb(f"pO{qg % 2}")
                S.op("dv", lambda e, po=po: e.reciprocal(rl[:, :], po[64:65, 0:512]),
                     reads=(pob,), writes=(gb("rl"),), tag=f"RL_{h}_{qg}")
                S.op("dv", lambda e: e.tensor_copy(rlb[:, :], rl[:, :]),
                     reads=(gb("rl"),), writes=(gb("rlb"),), tag=f"RLb_{h}_{qg}")
                S.op("pe", lambda e: e.matmul(pB[0:64, 0:512], ones_s[0:1, 0:64], rlb[0:1, :],
                                              start=True, stop=True),
                     reads=(gb("rlb"), gb("ones")), writes=(gb("pB"),), tag=f"RB_{h}_{qg}")
                S.op("ac", lambda e: e.activation(rb[:, :], pB[0:64, 0:512], AF.Copy),
                     reads=(gb("pB"),), writes=(gb("rb"),), tag=f"RBe_{h}_{qg}")
                S.op("dv", lambda e, h=h, hc=hc, qg=qg, po=po: e.tensor_mul(
                        oall[(h % 2) * 64:(h % 2) * 64 + 64,
                             hc * T + qg * 512:hc * T + (qg + 1) * 512],
                        po[0:64, 0:512], rb[:, :]),
                     reads=(pob, gb("rb")), writes=(gb("oall"),), tag=f"ON_{h}_{qg}")

            def emit_EPV(j, h=h, hb=hb):
                qg, kt = stages[j]
                nkt = 4 * (qg + 1)
                ps = pS[j % 3]
                pth = (j % 2) * 512
                po = pO[qg % 2]
                S.op("ac", lambda e, ps=ps, pth=pth: e.activation(
                        pt[:, pth:pth + 512], ps[:, 0:512], AF.Exp, bias=nmc[:, 0:1]),
                     reads=(gb(f"pS{j % 3}"), gb("nmc")), writes=(gb(f"pt{j % 2}"),),
                     tag=f"E_{h}_{qg}_{kt}")
                d = kt - 4 * qg
                if d >= 0:
                    S.op("gp", lambda e, d=d, pth=pth: e.tensor_mul(
                            pt[:, pth:pth + 512], pt[:, pth:pth + 512],
                            cm_s[:, d * 512:(d + 1) * 512]),
                         reads=(gb(f"pt{j % 2}"), gb("cm")), writes=(gb(f"pt{j % 2}"),),
                         tag=f"CM_{h}_{qg}_{kt}")
                S.op("pe", lambda e, hb=hb, kt=kt, nkt=nkt, pth=pth, po=po: e.matmul(
                        po[0:65, 0:512],
                        vaug[:, hb * 2080 + kt * 65:hb * 2080 + (kt + 1) * 65],
                        pt[:, pth:pth + 512], start=(kt == 0), stop=(kt == nkt - 1)),
                     reads=(gb(f"v{hb}"), gb(f"pt{j % 2}")), writes=(gb(f"pO{qg % 2}"),),
                     tag=f"PV_{h}_{qg}_{kt}")
                if kt == nkt - 1:
                    emit_tail(qg)

            for j in range(len(stages)):
                emit_S(j)
                if j > 0:
                    emit_EPV(j - 1)
            emit_EPV(len(stages) - 1)
            S.op("gp", lambda e: e.engine_nop(),
                 reads=(gb(f"v{hb}"),), writes=(gb(f"vfree{hb}"),), tag=f"freev{h}")
        S.op("gp", lambda e: e.engine_nop(),
             reads=(gb(f"qT{pb_}"), gb(f"kT{pb_}")),
             writes=(gb(f"qTfree{pb_}"), gb(f"kTfree{pb_}")), tag=f"free{hp}")

    for i in range(NT):
        tg = f"w{i}"
        if kind == "ffn":
            S.op("sd", lambda e, i=i: e.dma_start(out=xn1b[:, :],
                                                  in_=dxn1[i * 128:(i + 1) * 128, :]),
                 writes=(gb("xn1b"),), tag=f"ldxn1_{tg}")
        for half in range(2):
            pp = pW[half]
            ppb = gb(f"pW{half}")
            for hc in range(8):
                S.op("pe", lambda e, i=i, half=half, hc=hc, pp=pp: e.matmul(
                        pp[:, 0:512], oall[:, hc * T + i * 128:hc * T + (i + 1) * 128],
                        wo_s[:, hc * C + half * 512:hc * C + (half + 1) * 512],
                        start=(hc == 0), stop=(hc == 7)),
                     reads=(gb("oall"), gb("wo_s")), writes=(ppb,),
                     tag=f"MW{half}_{hc}_{tg}")
        if kind == "main":
            p9 = i % 2
            s9 = str(p9)
            c9 = p9 * C
            S.op("sd", lambda e, i=i, c9=c9: e.dma_start(
                    out=x1b9[:, c9:c9 + C], in_=dx1f[i * 128:(i + 1) * 128, :]),
                 writes=(gb("x19" + s9),), tag=f"lx9_{tg}")
            S.op("sd", lambda e, i=i, c9=c9: e.dma_start(
                    out=g1b9[:, c9:c9 + C], in_=dg1f[i * 128:(i + 1) * 128, :]),
                 writes=(gb("g19" + s9),), tag=f"lg9_{tg}")
            for half in range(2):
                S.op("ac", lambda e, half=half: e.activation(
                        ob[:, half * 512:(half + 1) * 512], pW[half][:, 0:512], AF.Copy),
                     reads=(gb(f"pW{half}"),), writes=(gb("ob"),), tag=f"EW{half}_{tg}")
            S.op("dv", lambda e, c9=c9: e.tensor_mul(ob9[:, c9:c9 + C], ob[:, :],
                                                     g1b9[:, c9:c9 + C]),
                 reads=(gb("ob"), gb("g19" + s9)), writes=(gb("ob9" + s9),),
                 tag=f"m9_{tg}")
            S.op("dv", lambda e, c9=c9: e.tensor_add(ob9[:, c9:c9 + C], ob9[:, c9:c9 + C],
                                                     x1b9[:, c9:c9 + C]),
                 reads=(gb("ob9" + s9), gb("x19" + s9)), writes=(gb("ob9" + s9),),
                 tag=f"a9_{tg}")
            S.op("gd", lambda e, i=i, c9=c9: e.dma_start(
                    out=dout[i * 128:(i + 1) * 128, :], in_=ob9[:, c9:c9 + C]),
                 reads=(gb("ob9" + s9),), writes=(gb("st1"),), tag=f"st_{tg}")
        else:
            for half in range(2):
                S.op("ac", lambda e, half=half: e.activation(
                        aot[:, half * 512:(half + 1) * 512], pW[half][:, 0:512], AF.Copy),
                     reads=(gb(f"pW{half}"),), writes=(gb("aot"),), tag=f"EW{half}_{tg}")
            S.op("gp", lambda e: e.tensor_mul(aot[:, :], aot[:, :], zms[:, :]),
                 reads=(gb("aot"), gb("zms")), writes=(gb("aot"),), tag=f"ZM_{tg}")
            S.op("dv", lambda e: e.tensor_add(xib[:, :], aot[:, :], xn1b[:, :]),
                 reads=(gb("aot"), gb("xn1b")), writes=(gb("xib"),), tag=f"XI_{tg}")
            S.op("dv", lambda e: e.tensor_copy(xibf[:, :], xib[:, :]),
                 reads=(gb("xib"),), writes=(gb("xibf"),), tag=f"XIb_{tg}")
            S.op("gd", lambda e, i=i: e.dma_start(out=dxi[i * 128:(i + 1) * 128, :],
                                                  in_=xibf[:, :]),
                 reads=(gb("xibf"),), writes=(gb("stxi"),), tag=f"sxi_{tg}")
            S.op("ac", lambda e: e.activation(sqs[:, :], xib[:, :], AF.Square,
                                              accum_out=stx[:, 0:1]),
                 reads=(gb("xib"),), writes=(gb("stx"), gb("sqs")), tag=f"n2a_{tg}")
            S.op("dv", lambda e: e.tensor_scalar(stx[:, 1:2], stx[:, 0:1], 1.0 / C,
                                                 EPS_RMS, ALU.mult, ALU.add),
                 reads=(gb("stx"),), writes=(gb("stx"),), tag=f"n2b_{tg}")
            S.op("ac", lambda e: e.activation(stx[:, 1:2], stx[:, 1:2], AF.Sqrt),
                 reads=(gb("stx"),), writes=(gb("stx"),), tag=f"n2c_{tg}")
            S.op("dv", lambda e: e.reciprocal(stx[:, 2:3], stx[:, 1:2]),
                 reads=(gb("stx"),), writes=(gb("stx"),), tag=f"n2d_{tg}")
            S.op("dv", lambda e: e.tensor_scalar_mul(sqs[:, :], xib[:, :], stx[:, 2:3]),
                 reads=(gb("xib"), gb("stx")), writes=(gb("sqs"),), tag=f"n2e_{tg}")
            S.op("gp", lambda e: e.tensor_mul(xfb[:, :], sqs[:, :], n2s[:, :]),
                 reads=(gb("sqs"), gb("n2s")), writes=(gb("xfb"),), tag=f"n2f_{tg}")
            S.op("gd", lambda e, i=i: e.dma_start(out=dxf[i * 128:(i + 1) * 128, :],
                                                  in_=xfb[:, :]),
                 reads=(gb("xfb"),), writes=(gb("stxf"),), tag=f"sxf_{tg}")
    return (nc, S) if standalone else None


def _prep_attn_inputs(inputs, kind, r5o, xn1_bf=None, x1=None, g1=None):
    hc = host_constants()
    qn, kn_, vn = (("qff", "kff", "vff") if kind == "ffn" else ("qma", "kma", "vma"))
    d = {
        qn: r5o[qn],
        kn_: r5o[kn_],
        vn: r5o[vn],
        "mstT": np.ascontiguousarray(r5o["mstT"]),
        "wo_t": np.ascontiguousarray(
            (inputs["L1_ffn_wo"] if kind == "ffn" else inputs["L1_attn_Wo"]).T),
        "cmask": _causal_masks(),
        "onesb": np.ones((1, 128), np.float32).astype(BF),
    }
    if kind == "main":
        d["x1"] = x1
        d["g1_bf"] = g1
    if kind == "ffn":
        zm = np.ones((128, C), np.float32)
        zm[:, 0:256] = 0.0
        d["xn1_bf"] = xn1_bf
        d["n2_rep"] = np.ascontiguousarray(
            np.broadcast_to(inputs["L1_ffn_n2_w"].astype(np.float32)[None, :], (128, C)))
        d["zmask"] = zm
    return d



def build_l7(cx=None, wbf=None):
    standalone = cx is None
    if standalone:
        cx = Ctx(bass.Bass("TRN2", target_bir_lowering=False), False)
        cx.S = Sched()
        cx.phase("l7")
    nc = cx.nc
    dxf = cx.dram_inter("xf_bf", [T, C], BF16, "in")
    dxi = cx.dram_inter("xi_bf", [T, C], BF16, "in")
    dgw = cx.dram_in("gw_t", [C, 4], F32)
    wnames = ["e0_up_t", "e0_down_t", "e1_w1_t", "e1_w2_t", "e1_down_t",
              "e2_l1_t", "e2_l2_t", "e2_l3_t", "e2_l4_t",
              "e3_down_t", "e3_up_t", "e3_out_t"]
    wshape = {"e0_up_t": (C, 2048), "e0_down_t": (2048, C), "e1_w1_t": (C, 2048),
              "e1_w2_t": (C, 2048), "e1_down_t": (2048, C), "e2_l1_t": (C, 2048),
              "e2_l2_t": (2048, 2048), "e2_l3_t": (2048, 2048), "e2_l4_t": (2048, C),
              "e3_down_t": (C, C), "e3_up_t": (C, 2048), "e3_out_t": (2048, C)}
    if wbf is None:
        dwn = {nm: cx.dram_in(nm, list(wshape[nm]), F32) for nm in wnames}
    else:
        dwn = wbf
    dzm = cx.dram_in("zmask", [128, C], F32)
    dg1 = cx.dram_inter("g1_bf", [T, C], BF16, "out")

    TB = 1024
    NB7 = T // TB
    A = cx.sb
    xfT = A("xfT", [128, 8 * T], BF16)
    gw_s = A("gw_s", [128, 8 * 4], BF16)
    wts = A("wts", [128, 4 * NT], F32)
    lbuf = A("lbuf", [128, 16], F32)
    u1 = A("u1", [128, 16 * TB], BF16)
    u2 = A("u2", [128, 16 * TB], BF16)
    macc = A("macc7", [128, 8 * C], F32)
    wblk = A("wblk", [128, 2 * 16 * 512], BF16)
    zms = A("zms", [128, C], F32)
    xib = A("xib7", [128, C], BF16)
    g1b = A("g1b", [128, C], F32)
    P = cx.psum
    pp_ = [P(f"pp{j}", [128, 512], F32) for j in range(4)]

    S = cx.S
    gb = cx.gb

    for oc in range(8):
        S.op("sd", lambda e, oc=oc: e.dma_start_transpose(
                xfT[:, oc * T:(oc + 1) * T], dxf[:, oc * 128:(oc + 1) * 128]),
             writes=(gb("xfT"),), tag=f"ld_xfT{oc}")
    S.op("gd", lambda e: e.dma_start(out=gw_s[:, :].rearrange("p (a o) -> p a o", o=4),
                                     in_=dgw.rearrange("(a p) o -> p a o", p=128)),
         writes=(gb("gw"),), tag="ld_gw")
    S.op("sd", lambda e: e.dma_start(out=zms[:, :], in_=dzm[:, :]),
         writes=(gb("zms"),), tag="ld_zm")

    for i in range(NT):
        tg = f"g{i}"
        pp = pp_[i % 2]
        ppb = gb(f"pp{i % 2}")
        for oc in range(8):
            S.op("pe", lambda e, i=i, oc=oc, pp=pp: e.matmul(
                    pp[0:128, 0:4], xfT[:, oc * T + i * 128:oc * T + (i + 1) * 128],
                    gw_s[:, oc * 4:(oc + 1) * 4], start=(oc == 0), stop=(oc == 7)),
                 reads=(gb("xfT"), gb("gw")), writes=(ppb,), tag=f"GM{oc}_{tg}")
        S.op("ac", lambda e, pp=pp: e.activation(lbuf[:, 0:4], pp[0:128, 0:4], AF.Copy),
             reads=(ppb,), writes=(gb("lb"),), tag=f"GE_{tg}")
        S.op("dv", lambda e: e.reduce_max(lbuf[:, 4:5], lbuf[:, 0:4], AX.X),
             reads=(gb("lb"),), writes=(gb("lb"),), tag=f"Gm1_{tg}")
        S.op("dv", lambda e: e.tensor_scalar_mul(lbuf[:, 5:6], lbuf[:, 4:5], -1.0),
             reads=(gb("lb"),), writes=(gb("lb"),), tag=f"Gneg_{tg}")
        S.op("ac", lambda e: e.activation(lbuf[:, 0:4], lbuf[:, 0:4], AF.Exp,
                                          bias=lbuf[:, 5:6]),
             reads=(gb("lb"),), writes=(gb("lb"),), tag=f"Gexp_{tg}")
        S.op("dv", lambda e: e.reduce_sum(lbuf[:, 6:7], lbuf[:, 0:4], AX.X),
             reads=(gb("lb"),), writes=(gb("lb"),), tag=f"Gs_{tg}")
        S.op("dv", lambda e: e.reciprocal(lbuf[:, 7:8], lbuf[:, 6:7]),
             reads=(gb("lb"),), writes=(gb("lb"),), tag=f"Grs_{tg}")
        S.op("dv", lambda e: e.tensor_scalar_mul(lbuf[:, 0:4], lbuf[:, 0:4], lbuf[:, 7:8]),
             reads=(gb("lb"),), writes=(gb("lb"),), tag=f"Gp_{tg}")
        S.op("dv", lambda e: e.reduce_max(lbuf[:, 8:9], lbuf[:, 0:4], AX.X),
             reads=(gb("lb"),), writes=(gb("lb"),), tag=f"Gm1p_{tg}")
        S.op("dv", lambda e: e.tensor_single_scalar(lbuf[:, 10:14], lbuf[:, 0:4],
                                                    lbuf[:, 8:9], ALU.is_ge),
             reads=(gb("lb"),), writes=(gb("lb"),), tag=f"Gmsk1_{tg}")
        S.op("dv", lambda e: e.scalar_tensor_tensor(lbuf[:, 10:14], lbuf[:, 10:14],
                                                    -1e30, lbuf[:, 0:4], ALU.mult, ALU.add),
             reads=(gb("lb"),), writes=(gb("lb"),), tag=f"Gpm_{tg}")
        S.op("dv", lambda e: e.reduce_max(lbuf[:, 9:10], lbuf[:, 10:14], AX.X),
             reads=(gb("lb"),), writes=(gb("lb"),), tag=f"Gm2_{tg}")
        S.op("dv", lambda e: e.tensor_single_scalar(lbuf[:, 10:14], lbuf[:, 0:4],
                                                    lbuf[:, 9:10], ALU.is_ge),
             reads=(gb("lb"),), writes=(gb("lb"),), tag=f"Gmsel_{tg}")
        S.op("dv", lambda e: e.tensor_tensor(lbuf[:, 14:15], lbuf[:, 8:9], lbuf[:, 9:10],
                                             ALU.add),
             reads=(gb("lb"),), writes=(gb("lb"),), tag=f"Gden_{tg}")
        S.op("dv", lambda e: e.tensor_scalar(lbuf[:, 14:15], lbuf[:, 14:15], 1.0, 1e-8,
                                             ALU.mult, ALU.add),
             reads=(gb("lb"),), writes=(gb("lb"),), tag=f"Gden2_{tg}")
        S.op("dv", lambda e: e.reciprocal(lbuf[:, 15:16], lbuf[:, 14:15]),
             reads=(gb("lb"),), writes=(gb("lb"),), tag=f"Grd_{tg}")
        S.op("dv", lambda e: e.tensor_mul(lbuf[:, 0:4], lbuf[:, 0:4], lbuf[:, 10:14]),
             reads=(gb("lb"),), writes=(gb("lb"),), tag=f"Gw1_{tg}")
        S.op("dv", lambda e, i=i: e.tensor_scalar_mul(wts[:, i * 4:(i + 1) * 4],
                                                      lbuf[:, 0:4], lbuf[:, 15:16]),
             reads=(gb("lb"),), writes=(gb("wts"),), tag=f"Gw2_{tg}")

    WHB = 16 * 512

    def wload(wname, ncols, sel, half, qeng, tg):
        if wbf is None:
            qeng = "gd"
        S.op(qeng, lambda e, wname=wname, ncols=ncols, sel=sel, half=half: e.dma_start(
                out=wblk[:, half * WHB:half * WHB + ncols * sel[1]]
                    .rearrange("p (a o) -> p a o", o=sel[1]),
                in_=dwn[wname].rearrange("(a p) o -> p a o", p=128)
                    [:, :, sel[0]:sel[0] + sel[1]]),
             writes=(gb(f"wblk{half}"),), tag=tg)

    def fm_layer(bi, wname, cin, cout, act, src_is_xf, src_s, src_buf, dst_s, dst_buf, tgp):
        kc = cin // 128
        noc = cout // 128
        wload(wname, kc, (0, 128), 0, "gd", f"W_{tgp}_0")
        for oc in range(noc):
            half = oc % 2
            if oc + 1 < noc:
                wload(wname, kc, ((oc + 1) * 128, 128), (oc + 1) % 2,
                      "gd" if (oc + 1) % 2 == 0 else "sd", f"W_{tgp}_{oc + 1}")
            for th in range(TB // 512):
                pp = pp_[(oc * 2 + th) % 4]
                ppb = gb(f"pp{(oc * 2 + th) % 4}")
                for a in range(kc):
                    if src_is_xf:
                        lh = lambda e, a=a, bi=bi, th=th: xfT[:, a * T + bi * TB + th * 512:
                                                             a * T + bi * TB + th * 512 + 512]
                    else:
                        lh = lambda e, a=a, th=th, src_s=src_s: src_s[:, a * TB + th * 512:
                                                                      a * TB + th * 512 + 512]
                    S.op("pe", lambda e, a=a, pp=pp, lh=lh, kc=kc, half=half: e.matmul(
                            pp[:, 0:512], wblk[:, half * WHB + a * 128:half * WHB + (a + 1) * 128],
                            lh(e), start=(a == 0), stop=(a == kc - 1)),
                         reads=(src_buf, gb(f"wblk{half}")), writes=(ppb,),
                         tag=f"M_{tgp}_{oc}_{th}_{a}")
                S.op("ac", lambda e, oc=oc, th=th, pp=pp, act=act, dst_s=dst_s: e.activation(
                        dst_s[:, oc * TB + th * 512:oc * TB + th * 512 + 512],
                        pp[:, 0:512], act),
                     reads=(ppb,), writes=(dst_buf,), tag=f"A_{tgp}_{oc}_{th}")

    def down_layer(bi, wname, cin, eidx, tgp, src_s=None, src_b=None):
        kc = cin // 128
        if src_s is None:
            src_s, src_b = u1, gb("u1")
        wload(wname, kc, (0, 512), 0, "gd", f"W_{tgp}_0")
        for ch in range(2):
            half = ch % 2
            if ch + 1 < 2:
                wload(wname, kc, (512, 512), 1, "sd", f"W_{tgp}_1")
            for tt in range(TB // 128):
                i_glob = bi * (TB // 128) + tt
                pp = pp_[tt % 4]
                ppb = gb(f"pp{tt % 4}")
                for a in range(kc):
                    S.op("pe", lambda e, a=a, tt=tt, pp=pp, src_s=src_s, half=half: e.matmul(
                            pp[:, 0:512], src_s[:, a * TB + tt * 128:a * TB + tt * 128 + 128],
                            wblk[:, half * WHB + a * 512:half * WHB + (a + 1) * 512],
                            start=(a == 0), stop=(a == kc - 1)),
                         reads=(src_b, gb(f"wblk{half}")), writes=(ppb,),
                         tag=f"M_{tgp}_{ch}_{tt}_{a}")
                S.op("dv", lambda e, ch=ch, tt=tt, pp=pp, i_glob=i_glob, eidx=eidx: (
                        e.scalar_tensor_tensor(
                            macc[:, tt * C + ch * 512:tt * C + ch * 512 + 512],
                            pp[:, 0:512], wts[:, i_glob * 4 + eidx:i_glob * 4 + eidx + 1],
                            macc[:, tt * C + ch * 512:tt * C + ch * 512 + 512],
                            ALU.mult, ALU.add) if eidx > 0 else
                        e.tensor_scalar_mul(
                            macc[:, tt * C + ch * 512:tt * C + ch * 512 + 512],
                            pp[:, 0:512], wts[:, i_glob * 4 + eidx:i_glob * 4 + eidx + 1])),
                     reads=(ppb, gb("wts"), gb("macc")), writes=(gb("macc"),),
                     tag=f"C_{tgp}_{ch}_{tt}")

    for bi in range(NB7):
        fm_layer(bi, "e0_up_t", C, 2048, AF.Gelu, True, xfT, gb("xfT"), u1, gb("u1"), f"b{bi}e0u")
        down_layer(bi, "e0_down_t", 2048, 0, f"b{bi}e0d")
        fm_layer(bi, "e1_w1_t", C, 2048, AF.Silu, True, xfT, gb("xfT"), u1, gb("u1"), f"b{bi}e1a")
        fm_layer(bi, "e1_w2_t", C, 2048, AF.Copy, True, xfT, gb("xfT"), u2, gb("u2"), f"b{bi}e1b")
        for sec in range(4):
            S.op("gp", lambda e, sec=sec: e.tensor_mul(
                    u1[:, sec * 4096:(sec + 1) * 4096], u1[:, sec * 4096:(sec + 1) * 4096],
                    u2[:, sec * 4096:(sec + 1) * 4096]),
                 reads=(gb("u1"), gb("u2")), writes=(gb("u1"),), tag=f"e1m{sec}_b{bi}")
        down_layer(bi, "e1_down_t", 2048, 1, f"b{bi}e1d")
        fm_layer(bi, "e2_l1_t", C, 2048, AF.Gelu, True, xfT, gb("xfT"), u2, gb("u2"), f"b{bi}e2a")
        fm_layer(bi, "e2_l2_t", 2048, 2048, AF.Gelu, False, u2, gb("u2"), u1, gb("u1"), f"b{bi}e2b")
        fm_layer(bi, "e2_l3_t", 2048, 2048, AF.Gelu, False, u1, gb("u1"), u2, gb("u2"), f"b{bi}e2c")
        down_layer(bi, "e2_l4_t", 2048, 2, f"b{bi}e2d", src_s=u2, src_b=gb("u2"))
        fm_layer(bi, "e3_down_t", C, C, AF.Gelu, True, xfT, gb("xfT"), u2, gb("u2"), f"b{bi}e3a")
        fm_layer(bi, "e3_up_t", C, 2048, AF.Gelu, False, u2, gb("u2"), u1, gb("u1"), f"b{bi}e3b")
        down_layer(bi, "e3_out_t", 2048, 3, f"b{bi}e3d")
        for tt in range(TB // 128):
            i_glob = bi * (TB // 128) + tt
            tg = f"o{i_glob}"
            S.op("sd", lambda e, i_glob=i_glob: e.dma_start(
                    out=xib[:, :], in_=dxi[i_glob * 128:(i_glob + 1) * 128, :]),
                 writes=(gb("xib"),), tag=f"ldxi_{tg}")
            S.op("gp", lambda e, tt=tt: e.tensor_mul(macc[:, tt * C:(tt + 1) * C],
                                                     macc[:, tt * C:(tt + 1) * C], zms[:, :]),
                 reads=(gb("macc"), gb("zms")), writes=(gb("macc"),), tag=f"zm_{tg}")
            S.op("dv", lambda e, tt=tt: e.tensor_add(g1b[:, :], macc[:, tt * C:(tt + 1) * C],
                                                     xib[:, :]),
                 reads=(gb("macc"), gb("xib")), writes=(gb("g1b"),), tag=f"ga_{tg}")
            S.op("ac", lambda e: e.activation(g1b[:, :], g1b[:, :], AF.Silu),
                 reads=(gb("g1b"),), writes=(gb("g1b"),), tag=f"gs_{tg}")
            S.op("gd", lambda e, i_glob=i_glob: e.dma_start(
                    out=dg1[i_glob * 128:(i_glob + 1) * 128, :], in_=g1b[:, :]),
                 reads=(gb("g1b"),), writes=(gb("stg"),), tag=f"sg_{tg}")
    return (nc, S) if standalone else None


def _prep_l7_inputs(inputs, xf_bf, xi_bf):
    zm = np.ones((128, C), np.float32)
    zm[:, 0:256] = 0.0
    d = {
        "xf_bf": xf_bf, "xi_bf": xi_bf,
        "gw_t": np.ascontiguousarray(inputs["L1_ffn_gate_w"].T.astype(np.float32)),
        "zmask": zm,
        "e0_up_t": np.ascontiguousarray(inputs["L1_ffn_e0_up"].T),
        "e0_down_t": np.ascontiguousarray(inputs["L1_ffn_e0_down"].T),
        "e1_w1_t": np.ascontiguousarray(inputs["L1_ffn_e1_w1"].T),
        "e1_w2_t": np.ascontiguousarray(inputs["L1_ffn_e1_w2"].T),
        "e1_down_t": np.ascontiguousarray(inputs["L1_ffn_e1_down"].T),
        "e2_l1_t": np.ascontiguousarray(inputs["L1_ffn_e2_l1"].T),
        "e2_l2_t": np.ascontiguousarray(inputs["L1_ffn_e2_l2"].T),
        "e2_l3_t": np.ascontiguousarray(inputs["L1_ffn_e2_l3"].T),
        "e2_l4_t": np.ascontiguousarray(inputs["L1_ffn_e2_l4"].T),
        "e3_down_t": np.ascontiguousarray(inputs["L1_ffn_e3_down"].T),
        "e3_up_t": np.ascontiguousarray(inputs["L1_ffn_e3_up"].T),
        "e3_out_t": np.ascontiguousarray(inputs["L1_ffn_e3_out"].T),
    }
    return d



def build_l9(cx=None):
    standalone = cx is None
    if standalone:
        cx = Ctx(bass.Bass("TRN2", target_bir_lowering=False), False)
        cx.S = Sched()
        cx.phase("l9")
    nc = cx.nc
    dx1 = cx.dram_inter("x1", [T, C], F32, "in")
    ds1 = cx.dram_inter("s1_bf", [T, C], BF16, "in")
    dg1 = cx.dram_inter("g1_bf", [T, C], BF16, "in")
    dout = cx.dram_inter("outp", [T, C], F32, "final")
    A = cx.sb
    x1b = A("x1b", [128, 2 * C], F32)
    s1b = A("s1b", [128, 2 * C], BF16)
    g1b = A("g1b", [128, 2 * C], BF16)
    ob = A("ob", [128, 2 * C], F32)
    S = cx.S
    gb = cx.gb
    for i in range(NT):
        b0 = (i % 2) * C
        tg = f"t{i}"
        S.op("sd", lambda e, i=i, b0=b0: e.dma_start(out=x1b[:, b0:b0 + C],
                                                     in_=dx1[i * 128:(i + 1) * 128, :]),
             writes=(gb("x1" + str(i % 2)),), tag=f"lx_{tg}")
        S.op("sd", lambda e, i=i, b0=b0: e.dma_start(out=s1b[:, b0:b0 + C],
                                                     in_=ds1[i * 128:(i + 1) * 128, :]),
             writes=(gb("s1" + str(i % 2)),), tag=f"ls_{tg}")
        S.op("sd", lambda e, i=i, b0=b0: e.dma_start(out=g1b[:, b0:b0 + C],
                                                     in_=dg1[i * 128:(i + 1) * 128, :]),
             writes=(gb("g1" + str(i % 2)),), tag=f"lg_{tg}")
        S.op("dv", lambda e, b0=b0: e.tensor_mul(ob[:, b0:b0 + C], s1b[:, b0:b0 + C],
                                                 g1b[:, b0:b0 + C]),
             reads=(gb("s1" + str(i % 2)), gb("g1" + str(i % 2))),
             writes=(gb("ob" + str(i % 2)),), tag=f"m_{tg}")
        S.op("dv", lambda e, b0=b0: e.tensor_add(ob[:, b0:b0 + C], ob[:, b0:b0 + C],
                                                 x1b[:, b0:b0 + C]),
             reads=(gb("ob" + str(i % 2)), gb("x1" + str(i % 2))),
             writes=(gb("ob" + str(i % 2)),), tag=f"a_{tg}")
        S.op("gd", lambda e, i=i, b0=b0: e.dma_start(out=dout[i * 128:(i + 1) * 128, :],
                                                     in_=ob[:, b0:b0 + C]),
             reads=(gb("ob" + str(i % 2)),), writes=(gb("st"),), tag=f"s_{tg}")
    return (nc, S) if standalone else None



_NC_CACHE = {}


def _get_progs():
    if _NC_CACHE:
        return _NC_CACHE
    _NC_CACHE["l1"] = build_l1()
    _NC_CACHE["l2"] = build_l2()
    nc3, S3 = build_l3()
    run_sched(nc3, S3)
    _NC_CACHE["l3"] = nc3
    for nm, bf in (("l4", build_l4), ("l5", build_l5), ("l7", build_l7)):
        ncx, Sx = bf()
        run_sched(ncx, Sx)
        _NC_CACHE[nm] = ncx
    for kind in ("ffn", "main"):
        ncx, Sx = build_attn(kind)
        run_sched(ncx, Sx)
        _NC_CACHE["attn_" + kind] = ncx
    return _NC_CACHE


_FUSED = True


def kernel(**inputs):
    inputs = {k: np.asarray(v) for k, v in inputs.items()}
    cores = list(range(NCORE))
    if _FUSED:
        nc, cx = _get_fused()
        r = run_bass_kernel_spmd(nc, [_prep_fused_inputs(inputs, b, cx) for b in cores],
                                 core_ids=cores)
        return np.stack([np.asarray(r.results[b]["outp"]).astype(np.float32)
                         for b in cores])
    progs = _get_progs()

    r1 = run_bass_kernel_spmd(progs["l1"], [_prep_l1_inputs(inputs, b) for b in cores],
                              core_ids=cores)
    r2 = run_bass_kernel_spmd(
        progs["l2"],
        [_prep_l2_inputs(inputs, b, np.asarray(r1.results[b]["xn_bf"])) for b in cores],
        core_ids=cores)
    r3 = run_bass_kernel_spmd(
        progs["l3"],
        [_prep_l3_inputs(inputs, {k: np.asarray(r2.results[b][k]) for k in
                                  ("xq_fm", "xk_fm", "xv_fm", "lr_fm")}) for b in cores],
        core_ids=cores)
    r4 = run_bass_kernel_spmd(
        progs["l4"],
        [_prep_l4_inputs(inputs, b, {"y_tm": np.asarray(r3.results[b]["y_tm"]),
                                     "xn_bf": np.asarray(r1.results[b]["xn_bf"]),
                                     "a_bf": np.asarray(r1.results[b]["a_bf"])})
         for b in cores], core_ids=cores)
    r5 = run_bass_kernel_spmd(
        progs["l5"],
        [_prep_l5_inputs(inputs, np.asarray(r4.results[b]["xn1_bf"])) for b in cores],
        core_ids=cores)
    r5o = [{k: np.asarray(r5.results[b][k]) for k in
            ("qma", "kma", "vma", "qff", "kff", "vff", "mstT")} for b in cores]
    r6 = run_bass_kernel_spmd(
        progs["attn_ffn"],
        [_prep_attn_inputs(inputs, "ffn", r5o[b], np.asarray(r4.results[b]["xn1_bf"]))
         for b in cores], core_ids=cores)
    r7 = run_bass_kernel_spmd(
        progs["l7"],
        [_prep_l7_inputs(inputs, np.asarray(r6.results[b]["xf_bf"]),
                         np.asarray(r6.results[b]["xi_bf"])) for b in cores],
        core_ids=cores)
    r8 = run_bass_kernel_spmd(
        progs["attn_main"],
        [_prep_attn_inputs(inputs, "main", r5o[b], x1=np.asarray(r4.results[b]["x1"]),
                           g1=np.asarray(r7.results[b]["g1_bf"])) for b in cores],
        core_ids=cores)
    out = np.stack([np.asarray(r8.results[b]["outp"]).astype(np.float32) for b in cores])
    return out



import contextlib


class Ctx:
    def __init__(self, nc, fused, pfx="", expose=False):
        self.nc = nc
        self.S = None
        self.fused = fused
        self.expose = expose
        self.pfx = pfx
        self.inter = {}
        self.bufs = {}
        self.stack = None
        self.inputs_map = []
        self.barrier_ops = []

    def gb(self, name):
        key = self.pfx + name
        if key not in self.bufs:
            self.bufs[key] = Buf(key)
        return self.bufs[key]

    def dram_in(self, name, shape, dt):
        ext = (self.pfx + name) if self.fused else name
        self.inputs_map.append((ext, name))
        return self.nc.dram_tensor(ext, shape, dt, kind="ExternalInput").ap()

    def dram_inter(self, name, shape, dt, role):
        if name in self.inter:
            return self.inter[name]
        if role == "final":
            kind = "ExternalOutput"
        elif role == "input":
            kind = "ExternalInput"
        elif self.fused and self.expose:
            kind = "ExternalOutput"
        else:
            kind = "Internal" if self.fused else (
                "ExternalInput" if role == "in" else "ExternalOutput")
        ap = self.nc.dram_tensor(name, shape, dt, kind=kind).ap()
        self.inter[name] = ap
        return ap

    def sb(self, name, shape, dt):
        h = self.stack.enter_context(self.nc.sbuf_tensor(self.pfx + name, shape, dt))
        return h

    def psum(self, name, shape, dt):
        h = self.stack.enter_context(self.nc.psum_tensor(self.pfx + name, shape, dt))
        return h

    def phase(self, pfx):
        if self.stack is not None:
            self.barrier()
            self.stack.close()
        self.stack = contextlib.ExitStack()
        self.pfx = pfx + "_"

    def barrier(self):
        last = {}
        for o in self.S.ops:
            last[o.eng] = o
        self.S.current_barrier = list(last.values())

    def finish(self):
        if self.stack is not None:
            self.stack.close()
            self.stack = None



def emit_l1(cx):
    nc, S, gb = cx.nc, cx.S, cx.gb
    x = cx.dram_inter("x", [T, C], F32, "input")
    wre = cx.dram_in("w_re", [C, C], F32)
    wim = cx.dram_in("w_im", [C, C], F32)
    dcos = cx.dram_in("dft_cos", [C, C], BF16)
    dsin = cx.dram_in("dft_sin", [C, C], BF16)
    xn_bf = cx.dram_inter("xn_bf", [T, C], BF16, "out")
    a_bf = cx.dram_inter("a_bf", [C, C], BF16, "out")
    A, P = cx.sb, cx.psum
    xb = A("xb", [128, 2 * C], F32)
    xnb = A("xnb", [128, 2 * C], BF16)
    sqs = A("sqs", [128, C], F32)
    stx = A("stx", [128, 16], F32)
    wre_s = A("wre_s", [128, 8 * C], BF16)
    wim_s = A("wim_s", [128, 8 * C], BF16)
    wren_s = A("wren_s", [128, 8 * C], BF16)
    cos_s = A("cos_s", [128, 8 * C], BF16)
    sin_s = A("sin_s", [128, 8 * C], BF16)
    pt_s = A("pt_s", [128, 8 * C], BF16)
    qt_s = A("qt_s", [128, 8 * C], BF16)
    a_s = A("a_s", [128, 1024], BF16)
    ps = [P("ps0", [128, 512], F32), P("ps1", [128, 512], F32)]

    S.op("sd", lambda e: e.dma_start(out=cos_s[:, :].rearrange("p (a o) -> p a o", o=C),
                                     in_=dcos.rearrange("(a p) o -> p a o", p=128)),
         writes=(gb("cos"),), tag="l1cos")
    S.op("sd", lambda e: e.dma_start(out=sin_s[:, :].rearrange("p (a o) -> p a o", o=C),
                                     in_=dsin.rearrange("(a p) o -> p a o", p=128)),
         writes=(gb("sin"),), tag="l1sin")
    S.op("gd", lambda e: e.dma_start(out=wre_s[:, :].rearrange("p (a o) -> p a o", o=C),
                                     in_=wre.rearrange("(a p) o -> p a o", p=128)),
         writes=(gb("wre"),), tag="l1wre")
    S.op("gd", lambda e: e.dma_start(out=wim_s[:, :].rearrange("p (a o) -> p a o", o=C),
                                     in_=wim.rearrange("(a p) o -> p a o", p=128)),
         writes=(gb("wim"),), tag="l1wim")
    S.op("dv", lambda e: e.tensor_scalar_mul(wren_s[:, :], wre_s[:, :], -1.0),
         reads=(gb("wre"),), writes=(gb("wren"),), tag="l1wren")

    for i in range(NT):
        b0 = (i % 2) * C
        tg = f"l1t{i}"
        S.op("sd", lambda e, i=i, b0=b0: e.dma_start(
                out=xb[:, b0:b0 + C], in_=x[i * 128:(i + 1) * 128, :]),
             writes=(gb("xb" + str(i % 2)),), tag=f"lx_{tg}")
        rb_ = gb("xb" + str(i % 2))
        S.op("ac", lambda e, b0=b0: e.activation(sqs[:, :], xb[:, b0:b0 + C], AF.Square,
                                                 accum_out=stx[:, 0:1]),
             reads=(rb_,), writes=(gb("stx"), gb("sqs")), tag=f"sq_{tg}")
        S.op("dv", lambda e, b0=b0: e.reduce_sum(stx[:, 4:5], xb[:, b0:b0 + C], AX.X),
             reads=(rb_,), writes=(gb("stx"),), tag=f"s1_{tg}")
        S.op("dv", lambda e: e.tensor_scalar_mul(stx[:, 1:2], stx[:, 4:5], 1.0 / C),
             reads=(gb("stx"),), writes=(gb("stx"),), tag=f"mu_{tg}")
        S.op("dv", lambda e: e.tensor_scalar(stx[:, 2:3], stx[:, 1:2], stx[:, 1:2],
                                             -EPS_LN, ALU.mult, ALU.add),
             reads=(gb("stx"),), writes=(gb("stx"),), tag=f"m2_{tg}")
        S.op("dv", lambda e: e.scalar_tensor_tensor(stx[:, 2:3], stx[:, 0:1], 1.0 / C,
                                                    stx[:, 2:3], ALU.mult, ALU.subtract),
             reads=(gb("stx"),), writes=(gb("stx"),), tag=f"vr_{tg}")
        S.op("ac", lambda e: e.activation(stx[:, 2:3], stx[:, 2:3], AF.Sqrt),
             reads=(gb("stx"),), writes=(gb("stx"),), tag=f"sx_{tg}")
        S.op("dv", lambda e: e.reciprocal(stx[:, 3:4], stx[:, 2:3]),
             reads=(gb("stx"),), writes=(gb("stx"),), tag=f"rc_{tg}")
        S.op("dv", lambda e, b0=b0: e.tensor_scalar(xnb[:, b0:b0 + C], xb[:, b0:b0 + C],
                                                    stx[:, 1:2], stx[:, 3:4],
                                                    ALU.subtract, ALU.mult),
             reads=(rb_, gb("stx")), writes=(gb("xnb" + str(i % 2)),), tag=f"xn_{tg}")
        S.op("gd", lambda e, i=i, b0=b0: e.dma_start(
                out=xn_bf[i * 128:(i + 1) * 128, :], in_=xnb[:, b0:b0 + C]),
             reads=(gb("xnb" + str(i % 2)),), writes=(gb("stxn"),), tag=f"sx_{tg}d")
    for n in range(48):
        ph, j = n // 16, n % 16
        oc, half = j // 2, j % 2
        ppt = ps[n % 2]
        ppb = gb(f"l1p{n % 2}")
        if ph == 0:
            lA, lB, rA, rB = wre_s, wim_s, cos_s, sin_s
            rd = (gb("wre"), gb("wim"), gb("cos"), gb("sin"))
        elif ph == 1:
            lA, lB, rA, rB = wim_s, wren_s, cos_s, sin_s
            rd = (gb("wim"), gb("wren"), gb("cos"), gb("sin"))
        else:
            lA, lB, rA, rB = pt_s, qt_s, cos_s, sin_s
            rd = (gb("pt"), gb("qt"), gb("cos"), gb("sin"))

        def mmgrp(e, lA=lA, lB=lB, rA=rA, rB=rB, oc=oc, half=half, ppt=ppt):
            for kc in range(8):
                e.matmul(ppt[:, 0:512], lA[:, kc * C + oc * 128:kc * C + oc * 128 + 128],
                         rA[:, kc * C + half * 512:kc * C + (half + 1) * 512],
                         start=(kc == 0), stop=False)
            for kc in range(8):
                mm = e.matmul(ppt[:, 0:512], lB[:, kc * C + oc * 128:kc * C + oc * 128 + 128],
                              rB[:, kc * C + half * 512:kc * C + (half + 1) * 512],
                              start=False, stop=(kc == 7))
            return mm
        S.op("pe", mmgrp, reads=rd, writes=(ppb,), tag=f"l1A{n}")
        if ph == 0:
            S.op("ac", lambda e, j=j, ppt=ppt: e.activation(
                    pt_s[:, (j // 2) * C + (j % 2) * 512:(j // 2) * C + (j % 2 + 1) * 512],
                    ppt[:, :], AF.Copy),
                 reads=(ppb,), writes=(gb("pt"),), tag=f"l1Ev{n}")
        elif ph == 1:
            S.op("ac", lambda e, j=j, ppt=ppt: e.activation(
                    qt_s[:, (j // 2) * C + (j % 2) * 512:(j // 2) * C + (j % 2 + 1) * 512],
                    ppt[:, :], AF.Copy, scale=-1.0),
                 reads=(ppb,), writes=(gb("qt"),), tag=f"l1Ev{n}")
        else:
            S.op("ac", lambda e, j=j, ppt=ppt: e.activation(
                    a_s[:, (j % 2) * 512:(j % 2 + 1) * 512], ppt[:, :], AF.Copy,
                    scale=1.0 / C),
                 reads=(ppb,), writes=(gb("a_s" + str(j % 2)),), tag=f"l1Ev{n}")
            S.op("gd", lambda e, j=j: e.dma_start(
                    out=a_bf.rearrange("(dc p) o -> dc p o", p=128)[j // 2, :,
                                                                   (j % 2) * 512:(j % 2 + 1) * 512],
                    in_=a_s[:, (j % 2) * 512:(j % 2 + 1) * 512]),
                 reads=(gb("a_s" + str(j % 2)),), writes=(gb("sta"),), tag=f"l1As{n}")


def emit_l2(cx):
    nc, S, gb = cx.nc, cx.S, cx.gb
    cst = host_constants()
    xn = cx.dram_inter("xn_bf", [T, C], BF16, "in")
    wq = cx.dram_in("wq_t", [C, C], F32)
    wk = cx.dram_in("wk_t", [C, C], F32)
    wv = cx.dram_in("wv_t", [C, C], F32)
    lrw = cx.dram_in("lrw_t", [C, 16], F32)
    lrb = cx.dram_in("lrb", [16, 1], F32)
    rc = cx.dram_in("rope_cos", [128, 16], F32)
    rs = cx.dram_in("rope_sin", [128, 16], F32)
    xq_fm = cx.dram_inter("xq_fm", [C, T], BF16, "out")
    xk_fm = cx.dram_inter("xk_fm", [C, T], BF16, "out")
    xv_fm = cx.dram_inter("xv_fm", [C, T], BF16, "out")
    lr_fm = cx.dram_inter("lr_fm", [16, T], BF16, "out")
    A, P = cx.sb, cx.psum
    xnT = A("xnT", [128, 8 * T], BF16)
    wq_s = A("wq_s", [128, 8 * C], BF16)
    wk_s = A("wk_s", [128, 8 * C], BF16)
    wv_s = A("wv_s", [128, 8 * C], BF16)
    lrw_s = A("lrw_s", [128, 8 * 16], BF16)
    lrb_s = A("lrb_s", [16, 1], F32)
    rc_s = A("rc_s", [128, 16], F32)
    rs_s = A("rs_s", [128, 16], F32)
    shb = A("shb", [128, 512], F32)
    t1b = A("t1b", [128, 512], F32)
    t2b = A("t2b", [128, 512], F32)
    ob = A("ob", [128, 2 * 512], BF16)
    lrob = A("lrob", [16, 2 * 512], BF16)
    pj = [P(f"pj{j}", [128, 512], F32) for j in range(4)]

    for oc in range(8):
        S.op("sd", lambda e, oc=oc: e.dma_start_transpose(
                xnT[:, oc * T:(oc + 1) * T], xn[:, oc * 128:(oc + 1) * 128]),
             writes=(gb("xnT"),), tag=f"l2xnT{oc}")
    for d_, s_, tg in ((rc_s, rc, "rc"), (rs_s, rs, "rs"), (lrb_s, lrb, "lrbld")):
        S.op("sd", lambda e, d=d_, s=s_: e.dma_start(out=d[:, :], in_=s[:, :]),
             writes=(gb(tg),), tag="l2" + tg)
    for w_s, w, tg in ((wv_s, wv, "wv"), (wq_s, wq, "wqld"), (wk_s, wk, "wkld")):
        S.op("gd", lambda e, ws=w_s, w=w: e.dma_start(
                out=ws[:, :].rearrange("p (a o) -> p a o", o=C),
                in_=w.rearrange("(a p) o -> p a o", p=128)),
             writes=(gb(tg),), tag="l2" + tg)
    S.op("gd", lambda e: e.dma_start(out=lrw_s[:, :].rearrange("p (a o) -> p a o", o=16),
                                     in_=lrw.rearrange("(a p) o -> p a o", p=128)),
         writes=(gb("lrw"),), tag="l2lrw")

    jobs = [("v", oc, tg) for oc in range(8) for tg in range(8)]
    jobs += [(w, oc, tg) for w in ("q", "k") for oc in range(8) for tg in range(8)]
    jobs += [("lr", 0, tg) for tg in range(8)]
    for n, (kind, oc, tg) in enumerate(jobs):
        ppt = pj[n % 4]
        ppb = gb(f"l2p{n % 4}")
        wmap = {"q": (wq_s, "wqld"), "k": (wk_s, "wkld"), "v": (wv_s, "wv"),
                "lr": (lrw_s, "lrw")}
        w_s, wb = wmap[kind]

        def mmg(e, kind=kind, oc=oc, tg=tg, ppt=ppt, w_s=w_s):
            for kc in range(8):
                if kind == "lr":
                    mm = e.matmul(ppt[0:16, 0:512], w_s[:, kc * 16:(kc + 1) * 16],
                                  xnT[:, kc * T + tg * 512:kc * T + (tg + 1) * 512],
                                  start=(kc == 0), stop=(kc == 7))
                else:
                    mm = e.matmul(ppt[:, 0:512],
                                  w_s[:, kc * C + oc * 128:kc * C + oc * 128 + 128],
                                  xnT[:, kc * T + tg * 512:kc * T + (tg + 1) * 512],
                                  start=(kc == 0), stop=(kc == 7))
            return mm
        S.op("pe", mmg, reads=(gb("xnT"), gb(wb)), writes=(ppb,), tag=f"l2M{n}")
        tga = f"l2e{n}"
        if kind == "v":
            S.op("ac", lambda e, n=n, ppt=ppt: e.activation(
                    ob[:, (n % 2) * 512:(n % 2) * 512 + 512], ppt[:, :], AF.Copy),
                 reads=(ppb,), writes=(gb("ob" + str(n % 2)),), tag=tga)
            S.op("gd", lambda e, n=n, oc=oc, tg=tg: e.dma_start(
                    out=xv_fm[oc * 128:(oc + 1) * 128, tg * 512:(tg + 1) * 512],
                    in_=ob[:, (n % 2) * 512:(n % 2) * 512 + 512]),
                 reads=(gb("ob" + str(n % 2)),), writes=(gb("l2st"),), tag=tga + "s")
        elif kind == "lr":
            S.op("ac", lambda e, n=n, ppt=ppt: e.activation(
                    lrob[:, (n % 2) * 512:(n % 2) * 512 + 512], ppt[0:16, :], AF.Sigmoid,
                    bias=lrb_s[:, 0:1]),
                 reads=(ppb, gb("lrbld")), writes=(gb("lro" + str(n % 2)),), tag=tga)
            S.op("gd", lambda e, n=n, tg=tg: e.dma_start(
                    out=lr_fm[:, tg * 512:(tg + 1) * 512],
                    in_=lrob[:, (n % 2) * 512:(n % 2) * 512 + 512]),
                 reads=(gb("lro" + str(n % 2)),), writes=(gb("l2st"),), tag=tga + "s")
        else:
            swap = [i ^ 1 for i in range(32)]
            S.op("dv", lambda e, ppt=ppt, swap=swap: e.stream_shuffle(shb[:, :], ppt[:, :],
                                                                      swap),
                 reads=(ppb,), writes=(gb("shb"),), tag=tga + "sh")
            S.op("dv", lambda e, ppt=ppt: e.tensor_mul(
                    t1b[:, :].rearrange("p (o t) -> p o t", t=16),
                    ppt[:, :].rearrange("p (o t) -> p o t", t=16),
                    rc_s[:, 0:16].rearrange("p (o t) -> p o t", o=1)
                        .broadcast_to([128, 32, 16])),
                 reads=(ppb, gb("rc")), writes=(gb("t1b"),), tag=tga + "c")
            S.op("gp", lambda e: e.tensor_mul(
                    t2b[:, :].rearrange("p (o t) -> p o t", t=16),
                    shb[:, :].rearrange("p (o t) -> p o t", t=16),
                    rs_s[:, 0:16].rearrange("p (o t) -> p o t", o=1)
                        .broadcast_to([128, 32, 16])),
                 reads=(gb("shb"), gb("rs")), writes=(gb("t2b"),), tag=tga + "t2")
            S.op("gp", lambda e, n=n: e.tensor_add(ob[:, (n % 2) * 512:(n % 2) * 512 + 512],
                                                   t1b[:, :], t2b[:, :]),
                 reads=(gb("t1b"), gb("t2b")), writes=(gb("ob" + str(n % 2)),), tag=tga + "a")
            S.op("gd", lambda e, n=n, oc=oc, tg=tg, kind=kind: e.dma_start(
                    out=(xq_fm if kind == "q" else xk_fm)[oc * 128:(oc + 1) * 128,
                                                          tg * 512:(tg + 1) * 512],
                    in_=ob[:, (n % 2) * 512:(n % 2) * 512 + 512]),
                 reads=(gb("ob" + str(n % 2)),), writes=(gb("l2st"),), tag=tga + "s")



_WSHAPE7 = {"e0_up_t": (C, 2048), "e0_down_t": (2048, C), "e1_w1_t": (C, 2048),
            "e1_w2_t": (C, 2048), "e1_down_t": (2048, C), "e2_l1_t": (C, 2048),
            "e2_l2_t": (2048, 2048), "e2_l3_t": (2048, 2048), "e2_l4_t": (2048, C),
            "e3_down_t": (C, C), "e3_up_t": (C, 2048), "e3_out_t": (2048, C)}


def prep_wcast(cx):
    S, gb = cx.S, cx.gb
    stage = cx.sb("wcstage", [128, 2 * 2048], BF16)
    wbf = {}
    jobs = []
    for nm, (cin, cout) in _WSHAPE7.items():
        src_ = cx.dram_in(nm, [cin, cout], F32)
        dst_ = cx.dram_inter(nm + "_bf", [cin, cout], BF16, "out")
        wbf[nm] = dst_
        for a in range(cin // 128):
            jobs.append((src_, dst_, a, cout))
    nb = NBLK - 1
    per = (len(jobs) + nb - 1) // nb

    def emit_batch(bi):
        if bi >= nb:
            return
        for j in range(bi * per, min((bi + 1) * per, len(jobs))):
            src_, dst_, a, cout = jobs[j]
            half = (j % 2) * 2048
            S.op("gd", lambda e, src_=src_, a=a, cout=cout, half=half: e.dma_start(
                    out=stage[:, half:half + cout], in_=src_[a * 128:(a + 1) * 128, :]),
                 reads=(gb(f"wcf{j % 2}"),), writes=(gb(f"wc{j % 2}"),), tag=f"wcl{j}")
            S.op("sd", lambda e, dst_=dst_, a=a, cout=cout, half=half: e.dma_start(
                    out=dst_[a * 128:(a + 1) * 128, :], in_=stage[:, half:half + cout]),
                 reads=(gb(f"wc{j % 2}"),), writes=(gb(f"wcf{j % 2}"),), tag=f"wcs{j}")
    return wbf, emit_batch


def _get_fused(expose=False):
    if "fused" not in _NC_CACHE:
        nc = bass.Bass("TRN2", target_bir_lowering=False)
        cx = Ctx(nc, True, expose=expose)
        cx.S = Sched()
        cx.phase("l1"); emit_l1(cx)
        cx.phase("l2"); emit_l2(cx)
        cx.phase("l3")
        cx.pfx = "wc_"
        wbf, wc_batch = prep_wcast(cx)
        cx.pfx = "l3_"
        build_l3(cx=cx, blk_hook=wc_batch)
        cx.phase("l4"); build_l4(cx=cx)
        cx.phase("l5"); build_l5(cx=cx)
        cx.phase("l6"); build_attn("ffn", cx=cx)
        cx.phase("l7"); build_l7(cx=cx, wbf=wbf)
        cx.phase("l8"); build_attn("main", cx=cx)
        cx.finish()
        run_sched(nc, cx.S)
        _NC_CACHE["fused"] = (nc, cx)
    return _NC_CACHE["fused"]


def _prep_fused_inputs(inputs, core, cx):
    dmy = np.zeros((2, 2), np.float32)
    p = {
        "l1": _prep_l1_inputs(inputs, core),
        "l2": _prep_l2_inputs(inputs, core, dmy),
        "l3": _prep_l3_inputs(inputs, {"xq_fm": dmy, "xk_fm": dmy, "xv_fm": dmy,
                                       "lr_fm": dmy}),
        "l4": _prep_l4_inputs(inputs, core, {"y_tm": dmy, "xn_bf": dmy, "a_bf": dmy}),
        "l5": _prep_l5_inputs(inputs, dmy),
        "l6": _prep_attn_inputs(inputs, "ffn", {"qff": dmy, "kff": dmy, "vff": dmy,
                                                "mstT": dmy}, dmy),
        "l7": _prep_l7_inputs(inputs, dmy, dmy),
        "l8": _prep_attn_inputs(inputs, "main", {"qma": dmy, "kma": dmy, "vma": dmy,
                                                 "mstT": dmy}, x1=dmy, g1=dmy),
        "wc": _prep_l7_inputs(inputs, dmy, dmy),
    }
    fused = {"x": p["l1"]["x"]}
    for ext, name in cx.inputs_map:
        fused[ext] = p[ext.split("_", 1)[0]][name]
    return fused
